# version 05 (final)
# baseline (speedup 1.0000x reference)
import numpy as np
import ml_dtypes
from contextlib import ExitStack
import concourse.bass as bass
import concourse.mybir as mybir
from concourse.bass_utils import run_bass_kernel_spmd

F32 = mybir.dt.float32
BF16 = mybir.dt.bfloat16
I32 = mybir.dt.int32
AF = mybir.ActivationFunctionType
ALU = mybir.AluOpType
AX = mybir.AxisListType.X

P = 128
D = 1024
T = 4096
DC = D // P
FLT_MIN = 1.17549435e-38
FLT_MAX = 3.4028234663852886e38

ENGINES = ("sync", "vector", "scalar", "tensor", "gpsimd")


class EM:

    def __init__(self, nc, es):
        self.nc = nc
        self.es = es
        self.prog = {e: [] for e in ENGINES}
        self.sems = {}
        self.cnt = {}
        self.dbg = None

    def sem(self, name):
        if name not in self.sems:
            self.sems[name] = self.es.enter_context(self.nc.semaphore(name))
            self.cnt[name] = 0
        return self.sems[name]

    def op(self, eng, method, *args, inc=None, **kw):
        s = None
        if inc:
            s = self.sem(inc)
            self.cnt[inc] += 1

        def fn(e, method=method, args=args, kw=kw, s=s):
            ins = getattr(e, method)(*args, **kw)
            if s is not None:
                ins.then_inc(s, 1)
        self.prog[eng].append(fn)
        return (inc, self.cnt[inc]) if inc else None

    def dma(self, q, out, in_):
        name = "dq_" + q
        s = self.sem(name)
        self.cnt[name] += 16
        tgt = self.cnt[name]

        def fn(e, out=out, in_=in_, s=s):
            e.dma_start(out=out, in_=in_).then_inc(s, 16)
        self.prog[q].append(fn)
        return (name, tgt)

    def wait(self, eng, tok):
        if tok is None:
            return
        name, val = tok
        if val <= 0:
            return
        s = self.sem(name)
        self.prog[eng].append(lambda e, s=s, v=val: e.wait_ge(s, v))

    def waits(self, eng, toks):
        best = {}
        for t in toks:
            if t is None:
                continue
            n, v = t
            best[n] = max(best.get(n, 0), v)
        for n, v in best.items():
            self.wait(eng, (n, v))

    def tok(self, name):
        return (name, self.cnt.get(name, 0))

    def assemble(self, block):
        for engname in ENGINES:
            body = self.prog[engname]
            if not body:
                continue

            def runner(e, body=body):
                for fn in body:
                    fn(e)
            getattr(block, engname)(runner)


class Arena:

    def __init__(self, nc, es, ncols_f32, name="arena"):
        self.h = es.enter_context(nc.sbuf_tensor(name, [P, ncols_f32], F32))
        self.hb = self.h.bitcast(BF16)
        self.hi = self.h.bitcast(I32)
        self.ncols = ncols_f32
        self.off = 0
        self.marks = []

    def _take(self, n):
        assert self.off + n <= self.ncols, \
            f"SBUF arena overflow: {self.off}+{n} > {self.ncols}"
        o = self.off
        self.off += n
        return o

    def f32(self, ncols):
        o = self._take(ncols)
        return self.h[:, o:o + ncols]

    def bf16(self, ncols):
        o = self._take((ncols + 1) // 2)
        return self.hb[:, 2 * o:2 * o + ncols]

    def i32(self, ncols):
        o = self._take(ncols)
        return self.hi[:, o:o + ncols]

    def push(self):
        self.marks.append(self.off)

    def pop(self):
        self.off = self.marks.pop()



def emit_stage1(em, ar, pb, cst, xin, xout, W):
    nc = em.nc
    TT = 512
    NTILE = T // TT

    ar.push()
    wqx = ar.bf16(16384)
    wqdt = ar.bf16(8192)
    xa = ar.f32(TT * DC)
    xna = ar.f32(TT * DC)
    ka = ar.bf16(TT * DC)
    ba = ar.f32(TT * DC)
    ca = ar.f32(TT * DC)
    ya = ar.f32(TT * DC)
    dta = ar.f32(TT * DC)
    bc_mu = ar.f32(TT)
    bc_r = ar.f32(TT)
    bc_s = ar.f32(TT)
    bc_rsx = ar.f32(TT)
    bc_rsdt = ar.f32(TT)
    w_s = ar.f32(TT)
    w_p = ar.f32(TT)
    w_pf = ar.f32(TT)
    w_psh = ar.f32(TT)
    w_rp = ar.f32(TT)
    w_g = ar.f32(TT)
    w_cs = ar.f32(TT)
    w_t0 = ar.f32(TT)
    w_t1 = ar.f32(TT)
    w_m = ar.i32(TT)
    zeros = ar.f32(TT)
    carry_S = ar.f32(DC)
    carry_PF = ar.f32(DC)
    carry_CS = ar.f32(DC)
    acc_abs = ar.f32(2)
    colw = ar.f32(8)
    a_col = ar.f32(DC)
    a_tmp = ar.f32(4 * DC)
    rowbuf = ar.f32(8)

    ones_c = cst["ones_c"]
    ones_r = cst["ones_r"]
    v = lambda *a, **k: em.op("vector", *a, **k)
    g = lambda *a, **k: em.op("gpsimd", *a, **k)
    s = lambda *a, **k: em.op("scalar", *a, **k)
    t = lambda *a, **k: em.op("tensor", *a, **k)

    v("memset", zeros, 0.0, inc="v")
    tk = em.dma("sync", a_tmp[:, 0:DC], W["Alog_c"])
    em.wait("scalar", tk)
    em.wait("scalar", s("activation", a_tmp[:, DC:2 * DC], a_tmp[:, 0:DC], AF.Abs,
                        inc="s"))
    em.wait("scalar", s("activation", a_tmp[:, 2 * DC:3 * DC], a_tmp[:, DC:2 * DC],
                        AF.Exp, scale=-1.0, inc="s"))
    s("activation", a_tmp[:, 3 * DC:4 * DC], a_tmp[:, 2 * DC:3 * DC], AF.Ln,
      bias=1.0, inc="s")
    em.wait("vector", em.tok("s"))
    em.wait("vector", v("tensor_relu", a_col, a_tmp[:, 0:DC], inc="v"))
    em.wait("vector", v("tensor_add", a_col, a_col, a_tmp[:, 3 * DC:4 * DC],
                        inc="v"))
    v("tensor_scalar_mul", a_col, a_col, -1.0, inc="v")

    def quant_weight(wT_dram, n_out, dst_bf16, acc_slot, inv_count, ws_slot):
        v("memset", acc_abs[:, acc_slot:acc_slot + 1], 0.0, inc="v")
        for c in range(DC):
            em.wait("sync", em.tok("v"))
            tk = em.dma("sync", xna[:, 0:n_out], wT_dram[c * P:(c + 1) * P, :])
            em.wait("vector", tk)
            em.wait("vector", v("tensor_reduce", w_t0[:, 0:1], xna[:, 0:n_out],
                                AX, ALU.add, apply_absolute_value=True, inc="v"))
            em.wait("vector", v("tensor_add", acc_abs[:, acc_slot:acc_slot + 1],
                                acc_abs[:, acc_slot:acc_slot + 1], w_t0[:, 0:1],
                                inc="v"))
        em.wait("tensor", em.tok("v"))
        t("matmul", pb[0][0:1, 0:1], ones_c, acc_abs[:, acc_slot:acc_slot + 1],
          start=True, stop=True, inc="t")
        em.wait("vector", em.tok("t"))
        em.wait("vector", v("tensor_scalar", rowbuf[0:1, 0:1], pb[0][0:1, 0:1],
                            inv_count, 1e-5, ALU.mult, ALU.add, inc="v"))
        em.wait("vector", v("reciprocal", rowbuf[0:1, 1:2], rowbuf[0:1, 0:1],
                            inc="v"))
        v("tensor_copy", rowbuf[0:1, 2:3], rowbuf[0:1, 0:1], inc="v")
        em.wait("tensor", em.tok("v"))
        t("matmul", pb[1][:, 0:2], ones_r, rowbuf[0:1, 1:3], start=True, stop=True,
          inc="t")
        em.wait("vector", em.tok("t"))
        em.wait("vector", v("tensor_copy", colw[:, ws_slot:ws_slot + 2],
                            pb[1][:, 0:2], inc="v"))
        for c in range(DC):
            em.wait("sync", em.tok("v"))
            tk = em.dma("sync", xna[:, 0:n_out], wT_dram[c * P:(c + 1) * P, :])
            em.wait("vector", tk)
            v("tensor_scalar_mul", ba[:, 0:n_out], xna[:, 0:n_out],
              colw[:, ws_slot:ws_slot + 1])
            v("tensor_scalar", ba[:, 0:n_out], ba[:, 0:n_out], -1.0, 1.0,
              ALU.max, ALU.min)
            v("tensor_copy", ca.bitcast(I32)[:, 0:n_out], ba[:, 0:n_out])
            v("tensor_copy", dst_bf16[:, c * n_out:(c + 1) * n_out],
              ca.bitcast(I32)[:, 0:n_out], inc="v")

    quant_weight(W["WxT"], 2048, wqx, 0, 1.0 / (2048 * 1024), 0)
    quant_weight(W["WdtT"], 1024, wqdt, 1, 1.0 / (1024 * 1024), 2)
    v("memset", carry_S, 0.0)
    v("memset", carry_PF, 1.0)
    v("memset", carry_CS, 0.0, inc="v")

    st_toks = []
    for it in range(NTILE):
        t0 = it * TT
        em.wait("tensor", em.tok("v"))
        em.wait("tensor", em.tok("g"))
        em.wait("tensor", em.tok("s"))
        em.wait("gpsimd", em.tok("v"))
        em.waits("scalar", st_toks)
        em.waits("vector", st_toks)
        if it > 0:
            em.wait("sync", em.tok("g"))
        ldtoks = []
        for c in range(DC):
            ldtoks.append(em.dma("sync", xa[:, c * TT:(c + 1) * TT],
                                 xin[c * P:(c + 1) * P, t0:t0 + TT]))
        em.waits("scalar", ldtoks)
        em.waits("tensor", ldtoks + [em.tok("v")])
        for c in range(DC):
            s("activation", ya[:, c * TT:(c + 1) * TT], xa[:, c * TT:(c + 1) * TT],
              AF.Square, inc=("s" if c == DC - 1 else None))
        for c in range(DC):
            t("matmul", pb[0][0:1, 0:TT], ones_c, xa[:, c * TT:(c + 1) * TT],
              start=(c == 0), stop=(c == DC - 1))
        em.wait("tensor", em.tok("s"))
        for c in range(DC):
            t("matmul", pb[1][0:1, 0:TT], ones_c, ya[:, c * TT:(c + 1) * TT],
              start=(c == 0), stop=(c == DC - 1), inc=("t" if c == DC - 1 else None))
        em.wait("vector", em.tok("t"))
        v("tensor_scalar_mul", w_t0[0:1, :], pb[0][0:1, 0:TT], 1.0 / D)
        v("tensor_scalar_mul", w_t1[0:1, :], pb[1][0:1, 0:TT], 1.0 / D)
        v("tensor_mul", w_g[0:1, :], w_t0[0:1, :], w_t0[0:1, :])
        v("tensor_sub", w_t1[0:1, :], w_t1[0:1, :], w_g[0:1, :])
        v("tensor_scalar_add", w_t1[0:1, :], w_t1[0:1, :], 1e-5, inc="v")
        em.wait("scalar", em.tok("v"))
        s("activation", w_cs[0:1, :], w_t1[0:1, :], AF.Sqrt, inc="s")
        em.wait("vector", em.tok("s"))
        v("reciprocal", w_t1[0:1, :], w_cs[0:1, :], inc="v")
        em.wait("tensor", em.tok("v"))
        t("matmul", pb[2][:, 0:TT], ones_r, w_t0[0:1, :], start=True, stop=True)
        t("matmul", pb[3][:, 0:TT], ones_r, w_t1[0:1, :], start=True, stop=True,
          inc="t")
        em.wait("vector", em.tok("t"))
        v("tensor_copy", bc_mu, pb[2][:, 0:TT])
        v("tensor_copy", bc_r, pb[3][:, 0:TT])
        for c in range(DC):
            sl = slice(c * TT, (c + 1) * TT)
            v("tensor_sub", xna[:, sl], xa[:, sl], bc_mu)
            v("tensor_mul", xna[:, sl], xna[:, sl], bc_r,
              inc=("v" if c == DC - 1 else None))
        em.wait("scalar", em.tok("v"))
        for c in range(DC):
            s("activation", ya[:, c * TT:(c + 1) * TT], xna[:, c * TT:(c + 1) * TT],
              AF.Abs, inc=("s" if c == DC - 1 else None))
        em.wait("vector", em.tok("s"))
        for c in range(1, DC):
            v("tensor_tensor", ya[:, 0:TT], ya[:, 0:TT], ya[:, c * TT:(c + 1) * TT],
              ALU.max)
        v("tensor_copy", w_t0[0:64, :], ya[64:128, 0:TT])
        v("tensor_tensor", ya[0:64, 0:TT], ya[0:64, 0:TT], w_t0[0:64, :], ALU.max)
        v("tensor_copy", w_t0[0:32, :], ya[32:64, 0:TT])
        v("tensor_tensor", ya[0:32, 0:TT], ya[0:32, 0:TT], w_t0[0:32, :], ALU.max)
        for k in (16, 8, 4, 2, 1):
            mask = [(i + k) % 32 for i in range(32)]
            v("stream_shuffle", ya[0:32, TT:2 * TT], ya[0:32, 0:TT], mask)
            v("tensor_tensor", ya[0:32, 0:TT], ya[0:32, 0:TT], ya[0:32, TT:2 * TT],
              ALU.max)
        v("tensor_scalar_max", ya[0:32, 0:TT], ya[0:32, 0:TT], 1e-5)
        v("reciprocal", ya[0:32, TT:2 * TT], ya[0:32, 0:TT])
        v("tensor_scalar_mul", bc_s[0:32, :], ya[0:32, TT:2 * TT], 127.0)
        for pbase in (32, 64, 96):
            v("tensor_copy", bc_s[pbase:pbase + 32, :], bc_s[0:32, :])
        v("reciprocal", bc_rsx, bc_s)
        v("tensor_scalar_mul", bc_rsdt, bc_rsx, colw[:, 3:4])
        v("tensor_scalar_mul", bc_rsx, bc_rsx, colw[:, 1:2])
        for c in range(DC):
            sl = slice(c * TT, (c + 1) * TT)
            v("tensor_mul", xna[:, sl], xna[:, sl], bc_s)
            v("tensor_copy", ya.bitcast(I32)[:, sl], xna[:, sl])
            v("tensor_copy", ka[:, sl], ya.bitcast(I32)[:, sl],
              inc=("v" if c == DC - 1 else None))
        ktok = em.tok("v")

        em.wait("tensor", ktok)
        evac_done = {}
        order = list(range(8, 16)) + list(range(16, 24)) + list(range(0, 8))
        for j2, oc in enumerate(order):
            bank = pb[4 + (j2 % 2)]
            if j2 >= 2:
                em.wait("tensor", evac_done[j2 - 2])
            for c in range(DC):
                if oc < 8:
                    stat = wqdt[:, c * 1024 + oc * P:c * 1024 + (oc + 1) * P]
                else:
                    o2 = oc - 8
                    stat = wqx[:, c * 2048 + o2 * P:c * 2048 + (o2 + 1) * P]
                t("matmul", bank[:, 0:TT], stat, ka[:, c * TT:(c + 1) * TT],
                  start=(c == 0), stop=(c == DC - 1),
                  inc=("t" if c == DC - 1 else None))
            em.wait("vector", em.tok("t"))
            ch = oc % 8
            sl = slice(ch * TT, (ch + 1) * TT)
            dst = dta if oc < 8 else (ba if oc < 16 else ca)
            scale = bc_rsdt if oc < 8 else bc_rsx
            v("tensor_tensor", dst[:, sl], bank[:, 0:TT], scale, ALU.mult,
              inc="v")
            evac_done[j2] = em.tok("v")

        em.wait("gpsimd", em.tok("v"))
        for ch in range(DC):
            sl = slice(ch * TT, (ch + 1) * TT)
            g("tensor_scalar_add", dta[:, sl], dta[:, sl],
              W["bdt_cols"][:, ch:ch + 1])
            g("tensor_scalar_add", ba[:, sl], ba[:, sl],
              W["bx_cols"][:, ch:ch + 1])
            g("tensor_scalar_add", ca[:, sl], ca[:, sl],
              W["bx_cols"][:, ch + 8:ch + 9],
              inc=("g" if ch == DC - 1 else None))
        em.wait("vector", em.tok("g"))
        v("tensor_scalar", ba[:, :], ba[:, :], -5.0, 5.0, ALU.max, ALU.min)
        v("tensor_scalar", ca[:, :], ca[:, :], -5.0, 5.0, ALU.max, ALU.min,
          inc="v")
        em.wait("scalar", em.tok("g"))
        em.wait("scalar", em.tok("v"))
        s("activation", ca[:, :], ca[:, :], AF.Tanh)
        s("activation", ya[:, :], dta[:, :], AF.Abs, inc="s")
        s("activation", ya[:, :], ya[:, :], AF.Exp, scale=-1.0)
        s("activation", ya[:, :], ya[:, :], AF.Ln, bias=1.0, inc="s")
        em.wait("vector", em.tok("s"))
        v("scalar_tensor_tensor", dta[:, :], dta[:, :], 0.0, ya[:, :],
          ALU.max, ALU.add)
        v("tensor_scalar", dta[:, :], dta[:, :], 0.001, 1.0, ALU.max, ALU.min)
        v("tensor_mul", ba[:, :], dta[:, :], ba[:, :])
        for ch in range(DC):
            sl = slice(ch * TT, (ch + 1) * TT)
            v("tensor_scalar", dta[:, sl], dta[:, sl], a_col[:, ch:ch + 1],
              -10.0, ALU.mult, ALU.max)
        v("tensor_scalar_min", dta[:, :], dta[:, :], 0.0, inc="v")

        if em.dbg is not None and it == 0:
            em.wait("gpsimd", em.tok("v"))
            em.wait("gpsimd", em.tok("s"))
            dtk = []
            for nm, buf in [("dt", dta), ("b", ba), ("tc", ca)]:
                for c in range(DC):
                    dtk.append(em.dma("gpsimd",
                                      em.dbg[nm][c * P:(c + 1) * P, 0:TT],
                                      buf[:, c * TT:(c + 1) * TT]))
            for c in range(DC):
                dtk.append(em.dma("gpsimd", em.dbg["k"][c * P:(c + 1) * P, 0:TT],
                                  ka[:, c * TT:(c + 1) * TT]))
            em.waits("vector", dtk)
        em.wait("vector", em.tok("s"))
        for c in range(DC):
            sl = slice(c * TT, (c + 1) * TT)
            cc = slice(c, c + 1)
            em.wait("vector", v("tensor_tensor_scan", ya[:, sl], dta[:, sl],
                                zeros, carry_S[:, cc], ALU.add, ALU.add,
                                inc="v"))
            v("tensor_copy", carry_S[:, cc], ya[:, c * TT + TT - 1:c * TT + TT],
              inc=("v" if c == DC - 1 else None))
        em.wait("scalar", em.tok("v"))
        s("activation", dta[:, :], ya[:, :], AF.Exp, inc="s")
        em.wait("vector", em.tok("s"))
        for c in range(DC):
            sl = slice(c * TT, (c + 1) * TT)
            cc = slice(c, c + 1)
            v("tensor_single_scalar", w_m, dta[:, sl], FLT_MIN, ALU.is_ge)
            v("select", w_pf, w_m, dta[:, sl], zeros)
            v("tensor_copy", w_psh[:, 0:1], carry_PF[:, cc])
            em.wait("vector", v("tensor_copy", w_psh[:, 1:TT], w_pf[:, 0:TT - 1],
                                inc="v"))
            em.wait("vector", v("tensor_copy", carry_PF[:, cc],
                                w_pf[:, TT - 1:TT], inc="v"))
            v("reciprocal", w_rp, w_psh)
            v("tensor_mul", w_g, ba[:, sl], w_rp)
            em.wait("vector", v("tensor_tensor_scan", w_cs, w_g, zeros,
                                carry_CS[:, cc], ALU.add, ALU.add, inc="v"))
            v("tensor_copy", carry_CS[:, cc], w_cs[:, TT - 1:TT])
            v("tensor_mul", w_t1, w_pf, w_cs)
            v("tensor_mul", xna[:, sl], w_t1, ca[:, sl],
              inc=("v" if c == DC - 1 else None))
            if em.dbg is not None and it == 0 and c == 0:
                em.wait("gpsimd", em.tok("v"))
                dtk = [em.dma("gpsimd", em.dbg["S"][0:P, 0:TT], w_s),
                       em.dma("gpsimd", em.dbg["Pf"][0:P, 0:TT], w_pf),
                       em.dma("gpsimd", em.dbg["cs"][0:P, 0:TT], w_cs)]
                em.waits("vector", dtk)
                em.waits("scalar", dtk)
        v("tensor_single_scalar", dta[:, :], xna[:, :], FLT_MAX, ALU.is_le)
        v("tensor_single_scalar", ba[:, :], xna[:, :], -FLT_MAX, ALU.is_ge)
        for c in range(DC):
            sl = slice(c * TT, (c + 1) * TT)
            v("tensor_tensor", w_m, dta[:, sl], ba[:, sl], ALU.mult)
            v("select", ya[:, sl], w_m, xna[:, sl], zeros,
              inc=("v" if c == DC - 1 else None))
        if em.dbg is not None and it == 0:
            em.wait("gpsimd", em.tok("v"))
            dtk = []
            for c in range(DC):
                dtk.append(em.dma("gpsimd", em.dbg["y"][c * P:(c + 1) * P, 0:TT],
                                  xna[:, c * TT:(c + 1) * TT]))
            em.waits("vector", dtk)
            em.waits("scalar", dtk)

        em.wait("scalar", em.tok("v"))
        for c in range(DC):
            s("activation", xna[:, c * TT:(c + 1) * TT], ya[:, c * TT:(c + 1) * TT],
              AF.Square, inc=("s" if c == DC - 1 else None))
        em.wait("tensor", em.tok("v"))
        for c in range(DC):
            t("matmul", pb[0][0:1, 0:TT], ones_c, ya[:, c * TT:(c + 1) * TT],
              start=(c == 0), stop=(c == DC - 1))
        em.wait("tensor", em.tok("s"))
        for c in range(DC):
            t("matmul", pb[1][0:1, 0:TT], ones_c, xna[:, c * TT:(c + 1) * TT],
              start=(c == 0), stop=(c == DC - 1), inc=("t" if c == DC - 1 else None))
        em.wait("vector", em.tok("t"))
        v("tensor_scalar_mul", w_t0[0:1, :], pb[0][0:1, 0:TT], 1.0 / D)
        v("tensor_scalar_mul", w_t1[0:1, :], pb[1][0:1, 0:TT], 1.0 / D)
        v("tensor_mul", w_g[0:1, :], w_t0[0:1, :], w_t0[0:1, :])
        v("tensor_sub", w_t1[0:1, :], w_t1[0:1, :], w_g[0:1, :])
        v("tensor_scalar_add", w_t1[0:1, :], w_t1[0:1, :], 1e-5, inc="v")
        em.wait("scalar", em.tok("v"))
        s("activation", w_cs[0:1, :], w_t1[0:1, :], AF.Sqrt, inc="s")
        em.wait("vector", em.tok("s"))
        v("reciprocal", w_t1[0:1, :], w_cs[0:1, :], inc="v")
        em.wait("tensor", em.tok("v"))
        t("matmul", pb[2][:, 0:TT], ones_r, w_t0[0:1, :], start=True, stop=True)
        t("matmul", pb[3][:, 0:TT], ones_r, w_t1[0:1, :], start=True, stop=True,
          inc="t")
        em.wait("vector", em.tok("t"))
        v("tensor_copy", bc_mu, pb[2][:, 0:TT])
        v("tensor_copy", bc_r, pb[3][:, 0:TT])
        st_toks = []
        for c in range(DC):
            sl = slice(c * TT, (c + 1) * TT)
            v("tensor_sub", ya[:, sl], ya[:, sl], bc_mu)
            v("tensor_mul", ya[:, sl], ya[:, sl], bc_r, inc="v")
            em.wait("gpsimd", em.tok("v"))
            em.wait("gpsimd", g("tensor_add", ya[:, sl], ya[:, sl], xa[:, sl],
                                inc="g"))
            st_toks.append(em.dma("gpsimd", xout[c * P:(c + 1) * P, t0:t0 + TT],
                                  ya[:, sl]))

    ar.pop()
    return st_toks





def emit_ln_stats(em, pb, cst, chunks, sq_chunks, rows, bc_mu, bc_r, TT,
                  wait_first=None):
    s = lambda *a, **k: em.op("scalar", *a, **k)
    v = lambda *a, **k: em.op("vector", *a, **k)
    t = lambda *a, **k: em.op("tensor", *a, **k)
    ones_c, ones_r = cst["ones_c"], cst["ones_r"]
    r0, r1, r2 = rows[0], rows[1], rows[2]
    for c in range(DC):
        s("activation", sq_chunks[c], chunks[c], AF.Square,
          inc=("s" if c == DC - 1 else None))
    if wait_first:
        em.waits("tensor", wait_first)
    em.wait("tensor", em.tok("v"))
    for c in range(DC):
        t("matmul", pb[0][0:1, 0:TT], ones_c, chunks[c],
          start=(c == 0), stop=(c == DC - 1))
    em.wait("tensor", em.tok("s"))
    for c in range(DC):
        t("matmul", pb[1][0:1, 0:TT], ones_c, sq_chunks[c],
          start=(c == 0), stop=(c == DC - 1), inc=("t" if c == DC - 1 else None))
    em.wait("vector", em.tok("t"))
    v("tensor_scalar_mul", r0[0:1, :], pb[0][0:1, 0:TT], 1.0 / D)
    v("tensor_scalar_mul", r1[0:1, :], pb[1][0:1, 0:TT], 1.0 / D)
    v("tensor_mul", r2[0:1, :], r0[0:1, :], r0[0:1, :])
    v("tensor_sub", r1[0:1, :], r1[0:1, :], r2[0:1, :])
    v("tensor_scalar_add", r1[0:1, :], r1[0:1, :], 1e-5, inc="v")
    em.wait("scalar", em.tok("v"))
    s("activation", r2[0:1, :], r1[0:1, :], AF.Sqrt, inc="s")
    em.wait("vector", em.tok("s"))
    v("reciprocal", r1[0:1, :], r2[0:1, :], inc="v")
    em.wait("tensor", em.tok("v"))
    t("matmul", pb[2][:, 0:TT], ones_r, r0[0:1, :], start=True, stop=True)
    t("matmul", pb[3][:, 0:TT], ones_r, r1[0:1, :], start=True, stop=True,
      inc="t")
    em.wait("vector", em.tok("t"))
    v("tensor_copy", bc_mu, pb[2][:, 0:TT])
    v("tensor_copy", bc_r, pb[3][:, 0:TT], inc="v")



def emit_stage2(em, ar, pb, cst, xin, xout, W, ud, hmd, hfd):
    TT = 512
    NT = T // TT
    v = lambda *a, **k: em.op("vector", *a, **k)
    g = lambda *a, **k: em.op("gpsimd", *a, **k)
    s = lambda *a, **k: em.op("scalar", *a, **k)
    t = lambda *a, **k: em.op("tensor", *a, **k)

    ar.push()
    xnb = ar.bf16(DC * T)
    xa = ar.f32(TT * DC)
    sq = ar.f32(TT * DC)
    bc_mu = ar.f32(TT)
    bc_r = ar.f32(TT)
    rows = [ar.f32(TT) for _ in range(3)]
    wp = [ar.bf16(2 * 1024) for _ in range(2)]
    wstg = [ar.f32(1024) for _ in range(2)]
    ustrip = ar.bf16(T)
    u1 = ar.f32(TT)
    u2 = ar.f32(TT)
    scl = ar.f32(64)
    tok = em.dma("sync", scl[:, 0:32], W["w1s_c"])
    tok = em.dma("sync", scl[:, 32:64], W["w2s_c"])
    em.wait("scalar", tok); em.wait("vector", tok)

    ldx_last = []
    for it in range(NT):
        t0 = it * TT
        ld = [em.dma("sync", xa[:, c * TT:(c + 1) * TT],
                     xin[c * P:(c + 1) * P, t0:t0 + TT]) for c in range(DC)]
        em.waits("scalar", ld)
        em.waits("tensor", ld)
        chunks = [xa[:, c * TT:(c + 1) * TT] for c in range(DC)]
        sqs = [sq[:, c * TT:(c + 1) * TT] for c in range(DC)]
        emit_ln_stats(em, pb, cst, chunks, sqs, rows, bc_mu, bc_r, TT)
        for c in range(DC):
            sl = slice(c * TT, (c + 1) * TT)
            v("tensor_sub", sq[:, sl], xa[:, sl], bc_mu)
            v("tensor_mul", sq[:, sl], sq[:, sl], bc_r)
            v("tensor_copy", xnb[:, c * T + t0:c * T + t0 + TT], sq[:, sl],
              inc=("v" if c == DC - 1 else None))
        em.wait("sync", em.tok("v"))
    xn_tok = em.tok("v")

    w1_3 = W["w1T"].rearrange("(c p) o -> p c o", p=P)
    w2_3 = W["w2T"].rearrange("(c p) o -> p c o", p=P)
    conv_tok = {}
    mm_tok = {}

    def pfA(o2):
        par = o2 % 2
        em.wait("sync", conv_tok.get(o2 - 2))
        if o2 - 2 in mm_tok:
            em.wait("vector", mm_tok[o2 - 2])
        else:
            em.wait("vector", em.tok("t"))
        for kk, wsrc in ((0, w1_3), (1, w2_3)):
            pl = em.dma("sync", wstg[par].rearrange("p (c j) -> p c j", j=P),
                        wsrc[:, :, o2 * P:(o2 + 1) * P])
            em.wait("vector", pl)
            v("tensor_copy", wp[par][:, kk * 1024:(kk + 1) * 1024],
              wstg[par][:, 0:1024], inc="v")
            if kk == 0:
                em.wait("sync", em.tok("v"))
        conv_tok[o2] = em.tok("v")

    pfA(0)
    pfA(1)
    for oc in range(32):
        wtok = conv_tok[oc]
        wp_ = wp[oc % 2]
        em.waits("tensor", [wtok, xn_tok, em.tok("s")])
        for tb in range(NT):
            t0 = tb * TT
            bank1 = pb[4 + (tb % 2)]
            bank2 = pb[0 + (tb % 2)]
            if tb >= 2:
                em.wait("tensor", ev[tb - 2])
                em.wait("tensor", evs[tb - 2])
            for c in range(DC):
                t("matmul", bank1[:, 0:TT], wp_[:, c * P:(c + 1) * P],
                  xnb[:, c * T + t0:c * T + t0 + TT],
                  start=(c == 0), stop=(c == DC - 1))
            for c in range(DC):
                t("matmul", bank2[:, 0:TT], wp_[:, 1024 + c * P:1024 + (c + 1) * P],
                  xnb[:, c * T + t0:c * T + t0 + TT],
                  start=(c == 0), stop=(c == DC - 1),
                  inc=("t" if c == DC - 1 else None))
            if tb == 0 and oc == 0:
                ev, evs = {}, {}
            em.wait("scalar", em.tok("t"))
            em.wait("scalar", em.tok("v"))
            s("activation", u1, bank1[:, 0:TT], AF.Silu,
              scale=scl[:, oc:oc + 1], inc="s")
            evs[tb] = em.tok("s")
            em.wait("vector", em.tok("t"))
            v("tensor_scalar_mul", u2, bank2[:, 0:TT], scl[:, 32 + oc:33 + oc],
              inc="v")
            em.wait("vector", em.tok("s"))
            v("tensor_mul", ustrip[:, t0:t0 + TT], u1, u2, inc="v")
            ev[tb] = em.tok("v")
        mm_tok[oc] = em.tok("t")
        if oc + 2 < 32:
            pfA(oc + 2)
        em.wait("gpsimd", em.tok("v"))
        sttok = em.dma("gpsimd", ud[oc * P:(oc + 1) * P, :], ustrip)
        em.wait("vector", sttok)
    em.wait("sync", em.tok("dq_gpsimd"))
    ar.pop()

    def hidden_layer(src_d, dst_d, wname, sname):
        ar.push()
        ut = ar.bf16(32 * 2048)
        wpan = [ar.bf16(32 * P) for _ in range(2)]
        wst = [ar.f32(2048) for _ in range(2)]
        ostrip = ar.bf16(2048)
        scl2 = ar.f32(32)
        tok = em.dma("sync", scl2, W[sname])
        em.wait("scalar", tok)
        em.wait("sync", em.tok("dq_gpsimd"))
        wsrc3 = W[wname].rearrange("(kc p) o -> p kc o", p=P)
        for tbg in range(2):
            tg0 = tbg * 2048
            ld = [em.dma("sync", ut[:, kc * 2048:(kc + 1) * 2048],
                         src_d[kc * P:(kc + 1) * P, tg0:tg0 + 2048])
                  for kc in range(32)]
            conv_tok = {}
            mm_tok = {}

            def pf(o2):
                par = o2 % 2
                em.wait("sync", conv_tok.get(o2 - 2))
                if o2 - 2 in mm_tok:
                    em.wait("vector", mm_tok[o2 - 2])
                else:
                    em.wait("vector", em.tok("t"))
                for half in range(2):
                    pl = em.dma(
                        "sync", wst[par].rearrange("p (kc j) -> p kc j", j=P),
                        wsrc3[:, half * 16:(half + 1) * 16, o2 * P:(o2 + 1) * P])
                    em.wait("vector", pl)
                    v("tensor_copy",
                      wpan[par][:, half * 2048:(half + 1) * 2048],
                      wst[par][:, 0:2048], inc="v")
                    if half == 0:
                        em.wait("sync", em.tok("v"))
                conv_tok[o2] = em.tok("v")

            pf(0)
            pf(1)
            ev = {}
            for oc in range(32):
                em.waits("tensor", [conv_tok[oc]] + ld + [em.tok("s")]
                         if oc == 0 else [conv_tok[oc]])
                wp_ = wpan[oc % 2]
                for ts4 in range(4):
                    t0 = ts4 * TT
                    bank = pb[4 + (ts4 % 2)]
                    key = (oc * 4 + ts4)
                    if key >= 2:
                        em.wait("tensor", ev.get(key - 2))
                    for kc in range(32):
                        t("matmul", bank[:, 0:TT], wp_[:, kc * P:(kc + 1) * P],
                          ut[:, kc * 2048 + t0:kc * 2048 + t0 + TT],
                          start=(kc == 0), stop=(kc == 31),
                          inc=("t" if kc == 31 else None))
                    em.wait("scalar", em.tok("t"))
                    s("activation", ostrip[:, t0:t0 + TT], bank[:, 0:TT], AF.Gelu,
                      scale=scl2[:, oc:oc + 1], inc="s")
                    ev[key] = em.tok("s")
                mm_tok[oc] = em.tok("t")
                if oc + 2 < 32:
                    pf(oc + 2)
                em.wait("gpsimd", em.tok("s"))
                sttok = em.dma("gpsimd", dst_d[oc * P:(oc + 1) * P, tg0:tg0 + 2048],
                               ostrip)
                em.wait("scalar", sttok)
            em.wait("sync", em.tok("t"))
        ar.pop()

    hidden_layer(ud, hmd, "h1T", "h1s_c")
    hidden_layer(hmd, hfd, "h2T", "h2s_c")

    ar.push()
    w3 = ar.bf16(32 * 1024)
    wst = ar.f32(2048)
    hft = ar.bf16(32 * TT)
    ya = ar.f32(TT * DC)
    sq = ar.f32(TT * DC)
    xa = ar.f32(TT * DC)
    bc_mu = ar.f32(TT)
    bc_r = ar.f32(TT)
    rows = [ar.f32(TT) for _ in range(3)]
    scl3 = ar.f32(8)
    em.wait("sync", em.tok("dq_gpsimd"))
    tok = em.dma("sync", scl3, W["w3s_c"])
    em.wait("vector", tok)
    for kc in range(32):
        em.wait("sync", em.tok("v"))
        tok = em.dma("sync", wst[:, 0:1024], W["w3T"][kc * P:(kc + 1) * P, :])
        em.wait("vector", tok)
        v("tensor_copy", w3[:, kc * 1024:(kc + 1) * 1024], wst[:, 0:1024], inc="v")
    w3tok = em.tok("v")
    em.wait("tensor", w3tok)
    st_toks = []
    for it in range(NT):
        t0 = it * TT
        em.wait("tensor", em.tok("v"))
        em.wait("tensor", em.tok("g"))
        em.waits("vector", st_toks)
        em.waits("scalar", st_toks)
        if it > 0:
            em.wait("sync", em.tok("g"))
        ldh = [em.dma("sync", hft[:, kc * TT:(kc + 1) * TT],
                      hfd[kc * P:(kc + 1) * P, t0:t0 + TT]) for kc in range(32)]
        ldx = [em.dma("sync", xa[:, c * TT:(c + 1) * TT],
                      xin[c * P:(c + 1) * P, t0:t0 + TT]) for c in range(DC)]
        em.waits("tensor", ldh)
        ev2 = {}
        for oc in range(DC):
            bank = pb[4 + (oc % 2)]
            if oc >= 2:
                em.wait("tensor", ev2[oc - 2])
            for kc in range(32):
                t("matmul", bank[:, 0:TT], w3[:, kc * 1024 + oc * P:kc * 1024 + (oc + 1) * P],
                  hft[:, kc * TT:(kc + 1) * TT], start=(kc == 0), stop=(kc == 31),
                  inc=("t" if kc == 31 else None))
            em.wait("vector", em.tok("t"))
            v("tensor_scalar_mul", ya[:, oc * TT:(oc + 1) * TT], bank[:, 0:TT],
              scl3[:, oc:oc + 1], inc="v")
            ev2[oc] = em.tok("v")
        if em.dbg is not None and it == 0:
            em.wait("gpsimd", em.tok("v"))
            ytk = []
            for c in range(DC):
                ytk.append(em.dma("gpsimd",
                                  em.dbg["y"][c * P:(c + 1) * P, 0:TT],
                                  ya[:, c * TT:(c + 1) * TT]))
            em.waits("vector", ytk)
        chunks = [ya[:, c * TT:(c + 1) * TT] for c in range(DC)]
        sqs = [sq[:, c * TT:(c + 1) * TT] for c in range(DC)]
        em.wait("scalar", em.tok("v"))
        emit_ln_stats(em, pb, cst, chunks, sqs, rows, bc_mu, bc_r, TT)
        em.waits("gpsimd", ldx)
        st_toks = []
        for c in range(DC):
            sl = slice(c * TT, (c + 1) * TT)
            v("tensor_sub", ya[:, sl], ya[:, sl], bc_mu)
            v("tensor_mul", ya[:, sl], ya[:, sl], bc_r, inc="v")
            em.wait("gpsimd", em.tok("v"))
            em.wait("gpsimd", g("tensor_add", ya[:, sl], ya[:, sl], xa[:, sl],
                                inc="g"))
            st_toks.append(em.dma("gpsimd", xout[c * P:(c + 1) * P, t0:t0 + TT],
                                  ya[:, sl]))
    ar.pop()
    return st_toks


def host_inputs_stage2(gg):
    h = {}
    h["w1T"] = np.ascontiguousarray(gg["L0_ffn_w1_w"].T)
    h["w2T"] = np.ascontiguousarray(gg["L0_ffn_w2_w"].T)
    h["h1T"] = np.ascontiguousarray(gg["L0_ffn_h1_w"].T)
    h["h2T"] = np.ascontiguousarray(gg["L0_ffn_h2_w"].T)
    h["w3T"] = np.ascontiguousarray(gg["L0_ffn_w3_w"].T)
    h["w1s_c"] = np.ascontiguousarray(gg["L0_ffn_w1_s"].reshape(32, P).T)
    h["w2s_c"] = np.ascontiguousarray(gg["L0_ffn_w2_s"].reshape(32, P).T)
    h["h1s_c"] = np.ascontiguousarray(gg["L0_ffn_h1_s"].reshape(32, P).T)
    h["h2s_c"] = np.ascontiguousarray(gg["L0_ffn_h2_s"].reshape(32, P).T)
    h["w3s_c"] = np.ascontiguousarray(gg["L0_ffn_w3_s"].reshape(DC, P).T)
    return h





def emit_stage3(em, ar, pb, pbT, cst, xin, xout, W):
    U = 256
    NST = T // U
    CH = 128
    NCH = U // CH
    v = lambda *a, **k: em.op("vector", *a, **k)
    g = lambda *a, **k: em.op("gpsimd", *a, **k)
    s = lambda *a, **k: em.op("scalar", *a, **k)
    t = lambda *a, **k: em.op("tensor", *a, **k)

    def vw(*a, **k):
        em.wait("vector", v(*a, inc="v", **k))

    ar.push()
    fb = ar.f32(8)

    def vfence():
        v("memset", fb[:, 0:1], 0.0, inc="v")
        em.wait("vector", em.tok("v"))

    def sfence():
        s("memzero", fb[:, 1:2], inc="s")
        em.wait("scalar", em.tok("s"))
    wq_sb = ar.bf16(DC * 1024)
    wk_sb = ar.bf16(DC * 1024)
    wv_sb = ar.bf16(DC * 1024)
    wo_sb = ar.bf16(DC * 1024)
    wbeta_sb = ar.bf16(DC * 16)
    x2t = ar.f32(U * DC)
    xs = ar.f32(U * DC)
    xo = ar.f32(U * DC)
    sq = ar.f32(U * DC)
    yo = ar.f32(U * DC)
    xob = ar.bf16(U * DC)
    knb = ar.bf16(U * DC)
    vnb = ar.bf16(U * DC)
    qb = ar.bf16(U * DC)
    kb = ar.bf16(U * DC)
    vb = ar.bf16(U * DC)
    ob = ar.bf16(U * DC)
    vt_all = ar.bf16(U * DC)
    kst = ar.bf16(DC * CH)
    qs = ar.bf16(DC * CH)
    ks = ar.bf16(DC * CH)
    asb = ar.bf16(2 * CH)
    bcq = ar.f32(DC * CH)
    bck = ar.f32(DC * CH)
    plbc = ar.f32(512)
    mst = ar.f32(512)
    mstb = ar.bf16(512)
    mtmp = ar.f32(512)
    beta = ar.f32(U)
    brow = ar.f32(8 * CH)
    bc_mu = ar.f32(U)
    bc_r = ar.f32(U)
    rows = [ar.f32(U) for _ in range(3)]
    zeros = ar.f32(512)
    cols = ar.f32(64)
    esel = ar.f32(1024)
    epar = ar.f32(128)
    d64 = ar.f32(512)
    triu = ar.f32(128)
    wstg = ar.f32(1024)

    toks = [em.dma("sync", esel[0:16, :], W["c_esel"]),
            em.dma("sync", epar[0:16, :], W["c_epar"]),
            em.dma("sync", d64[0:16, :], W["c_d64"]),
            em.dma("sync", triu, W["c_triu"]),
            em.dma("sync", cols[:, 0:8], W["rin_c"]),
            em.dma("sync", cols[:, 8:16], W["rk_c"]),
            em.dma("sync", cols[:, 16:24], W["rv_c"]),
            em.dma("sync", cols[:, 24:32], W["bo_c"]),
            em.dma("sync", cols[:, 32:40], W["lnw_c"]),
            em.dma("sync", cols[:, 40:48], W["lnb_c"]),
            em.dma("sync", cols[0:16, 48:49], W["betab_c"]),
            em.dma("sync", cols[0:16, 49:50], W["basebeta_c"])]
    em.waits("vector", toks)
    em.waits("scalar", toks)
    vw("memset", zeros, 0.0)
    vw("memset", mst, 0.0)
    vw("memset", mstb, 0.0)
    vw("memset", cols[:, 50:51], 0.0)
    em.wait("scalar", em.tok("v"))
    s("activation", cols[0:16, 51:52], cols[0:16, 49:50], AF.Sigmoid, inc="s")
    em.wait("vector", em.tok("s"))

    for wname, dst in (("WqT", wq_sb), ("WkT", wk_sb), ("WvT", wv_sb),
                       ("WoT", wo_sb)):
        for c in range(DC):
            em.wait("sync", em.tok("v"))
            tk = em.dma("sync", wstg, W[wname][c * P:(c + 1) * P, :])
            em.wait("vector", tk)
            v("tensor_copy", dst[:, c * 1024:(c + 1) * 1024], wstg, inc="v")
    for c in range(DC):
        em.wait("sync", em.tok("v"))
        tk = em.dma("sync", wstg[:, 0:16], W["betawT"][c * P:(c + 1) * P, :])
        em.wait("vector", tk)
        v("tensor_copy", wbeta_sb[:, c * 16:(c + 1) * 16], wstg[:, 0:16], inc="v")

    rin_c, rk_c, rv_c = cols[:, 0:8], cols[:, 8:16], cols[:, 16:24]
    bo_c, lnw_c, lnb_c = cols[:, 24:32], cols[:, 32:40], cols[:, 40:48]
    betab, sigbase, din = cols[0:16, 48:49], cols[0:16, 51:52], cols[0:16, 50:51]

    def rms_bc(src_chunks, eps):
        em.wait("scalar", em.tok("v"))
        for c in range(DC):
            s("activation", sq[:, c * U:(c + 1) * U], src_chunks[c], AF.Square,
              inc=("s" if c == DC - 1 else None))
        em.wait("tensor", em.tok("s"))
        em.wait("tensor", em.tok("v"))
        for c in range(DC):
            t("matmul", pb[0][0:1, 0:U], cst["ones_c"], sq[:, c * U:(c + 1) * U],
              start=(c == 0), stop=(c == DC - 1), inc=("t" if c == DC - 1 else None))
        em.wait("vector", em.tok("t"))
        v("tensor_scalar", rows[0][0:1, :], pb[0][0:1, 0:U], 1.0 / D, eps,
          ALU.mult, ALU.add, inc="v")
        em.wait("scalar", em.tok("v"))
        s("activation", rows[1][0:1, :], rows[0][0:1, :], AF.Sqrt, inc="s")
        em.wait("vector", em.tok("s"))
        v("reciprocal", rows[0][0:1, :], rows[1][0:1, :], inc="v")
        em.wait("tensor", em.tok("v"))
        t("matmul", pb[2][:, 0:U], cst["ones_r"], rows[0][0:1, :], start=True,
          stop=True, inc="t")
        em.wait("vector", em.tok("t"))
        v("tensor_copy", bc_r, pb[2][:, 0:U])

    st_toks = []
    for st in range(NST):
        t0 = st * U
        em.wait("tensor", em.tok("v"))
        em.wait("tensor", em.tok("g"))
        em.wait("tensor", em.tok("s"))
        em.waits("vector", st_toks)
        em.waits("scalar", st_toks)
        em.wait("gpsimd", em.tok("v"))
        if st > 0:
            em.wait("sync", em.tok("g"))
        ld = [em.dma("sync", x2t[:, c * U:(c + 1) * U],
                     xin[c * P:(c + 1) * P, t0:t0 + U]) for c in range(DC)]
        em.waits("scalar", ld)
        em.waits("tensor", ld)
        em.waits("vector", ld)
        chunks0 = [x2t[:, c * U:(c + 1) * U] for c in range(DC)]
        sqs0 = [sq[:, c * U:(c + 1) * U] for c in range(DC)]
        em.wait("scalar", em.tok("v"))
        emit_ln_stats(em, pb, cst, chunks0, sqs0, rows, bc_mu, bc_r, U)
        for c in range(DC):
            sl = slice(c * U, (c + 1) * U)
            v("tensor_sub", xs[:, sl], x2t[:, sl], bc_mu)
            v("tensor_mul", xs[:, sl], xs[:, sl], bc_r,
              inc=("v" if c == DC - 1 else None))
        rms_bc([xs[:, c * U:(c + 1) * U] for c in range(DC)], 1e-6)
        for c in range(DC):
            sl = slice(c * U, (c + 1) * U)
            v("tensor_mul", xo[:, sl], xs[:, sl], bc_r)
            v("tensor_scalar_mul", xo[:, sl], xo[:, sl], rin_c[:, c:c + 1])
            v("tensor_copy", xob[:, sl], xo[:, sl],
              inc=("v" if c == DC - 1 else None))
        rms_bc([xo[:, c * U:(c + 1) * U] for c in range(DC)], 1e-6)
        for c in range(DC):
            sl = slice(c * U, (c + 1) * U)
            v("tensor_mul", sq[:, sl], xo[:, sl], bc_r)
            v("tensor_scalar_mul", knb[:, sl], sq[:, sl], rk_c[:, c:c + 1])
            v("tensor_scalar_mul", vnb[:, sl], sq[:, sl], rv_c[:, c:c + 1],
              inc=("v" if c == DC - 1 else None))
        intok = em.tok("v")

        em.wait("tensor", intok)
        ev = {}
        j = 0
        for wsb, src_b, dst_b in ((wq_sb, xob, qb), (wk_sb, knb, kb),
                                  (wv_sb, vnb, vb)):
            for oc in range(DC):
                bank = pb[4 + (j % 2)]
                if j >= 2:
                    em.wait("tensor", ev[j - 2])
                for c in range(DC):
                    t("matmul", bank[:, 0:U],
                      wsb[:, c * 1024 + oc * P:c * 1024 + (oc + 1) * P],
                      src_b[:, c * U:(c + 1) * U], start=(c == 0),
                      stop=(c == DC - 1), inc=("t" if c == DC - 1 else None))
                em.wait("vector", em.tok("t"))
                v("tensor_copy", dst_b[:, oc * U:(oc + 1) * U], bank[:, 0:U],
                  inc="v")
                ev[j] = em.tok("v")
                j += 1
        bank = pb[4 + (j % 2)]
        em.wait("tensor", ev[j - 2])
        for c in range(DC):
            t("matmul", bank[0:16, 0:U], wbeta_sb[:, c * 16:(c + 1) * 16],
              xob[:, c * U:(c + 1) * U], start=(c == 0), stop=(c == DC - 1),
              inc=("t" if c == DC - 1 else None))
        em.wait("scalar", em.tok("t"))
        em.wait("scalar", em.tok("v"))
        s("activation", beta[0:16, :], bank[0:16, 0:U], AF.Sigmoid,
          bias=betab, inc="s")
        sfence()
        em.wait("vector", em.tok("s"))
        vw("tensor_scalar", beta[0:16, :], beta[0:16, :], sigbase, 0.9995,
           ALU.mult, ALU.min)
        vw("tensor_scalar_max", beta[0:16, :], beta[0:16, :], 0.9)

        em.wait("tensor", em.tok("v"))
        for ch in range(NCH):
            for c in range(DC):
                slot = (c % 4) * 256
                t("transpose", pbT[:, slot:slot + 128],
                  vb[:, c * U + ch * CH:c * U + ch * CH + CH], cst["identb"],
                  inc="t")
                em.wait("vector", em.tok("t"))
                v("tensor_copy",
                  vt_all[:, ch * 1024 + c * CH:ch * 1024 + (c + 1) * CH],
                  pbT[:, slot:slot + 128], inc="v")
                em.wait("tensor", em.tok("v"))
        vfence()

        for ch in range(NCH):
            cb = ch * CH
            bsl = beta[0:16, cb:cb + CH]
            r0 = brow[0:16, 0:CH]
            r1 = brow[0:16, CH:2 * CH]
            r2 = brow[0:16, 2 * CH:3 * CH]
            r3 = brow[0:16, 3 * CH:4 * CH]
            r4 = brow[0:16, 4 * CH:5 * CH]
            r5 = brow[0:16, 5 * CH:6 * CH]
            plc = brow[0:16, 6 * CH:6 * CH + 1]
            vw("tensor_tensor_scan", r0, bsl, zeros[0:16, 0:CH], 1.0, ALU.mult,
               ALU.add)
            vw("reciprocal", r1, r0)
            vw("tensor_tensor_scan", r2, r1, zeros[0:16, 0:CH], 0.0, ALU.add,
               ALU.add)
            vw("tensor_scalar_add", r3, r2, din)
            vw("tensor_mul", r3, r3, r0)
            vw("tensor_copy", din, r3[0:16, CH - 1:CH])
            vw("tensor_scalar_add", r5, r3, 1e-6)
            vw("reciprocal", r5, r5)
            vw("tensor_mul", r4, r5, r0)
            vw("tensor_copy", plc, r0[0:16, CH - 1:CH])
            em.wait("tensor", em.tok("v"))
            for c in range(DC):
                bank = pb[0 + c // 4]
                t("matmul", bank[:, (c % 4) * CH:(c % 4 + 1) * CH],
                  esel[0:16, c * P:(c + 1) * P], r4, start=True, stop=True,
                  inc=("t" if c == DC - 1 else None))
            em.wait("vector", em.tok("t"))
            v("tensor_copy", bcq[:, 0:512], pb[0][:, 0:512])
            v("tensor_copy", bcq[:, 512:1024], pb[1][:, 0:512], inc="v")
            em.wait("tensor", em.tok("v"))
            for c in range(DC):
                bank = pb[0 + c // 4]
                t("matmul", bank[:, (c % 4) * CH:(c % 4 + 1) * CH],
                  esel[0:16, c * P:(c + 1) * P], r1, start=True, stop=True,
                  inc=("t" if c == DC - 1 else None))
            em.wait("vector", em.tok("t"))
            v("tensor_copy", bck[:, 0:512], pb[0][:, 0:512])
            v("tensor_copy", bck[:, 512:1024], pb[1][:, 0:512], inc="v")
            vw("tensor_scalar_mul", mtmp[0:16, 0:512], d64[0:16, :], plc)
            vfence()
            em.wait("tensor", em.tok("v"))
            t("matmul", pb[0][:, 0:512], epar[0:16, :], mtmp[0:16, 0:512],
              start=True, stop=True, inc="t")
            em.wait("vector", em.tok("t"))
            v("tensor_copy", plbc, pb[0][:, 0:512], inc="v")
            for c in range(DC):
                sl = slice(c * U + cb, c * U + cb + CH)
                v("tensor_mul", qs[:, c * CH:(c + 1) * CH], qb[:, sl],
                  bcq[:, c * CH:(c + 1) * CH])
                v("tensor_mul", ks[:, c * CH:(c + 1) * CH], kb[:, sl],
                  bck[:, c * CH:(c + 1) * CH],
                  inc=("v" if c == DC - 1 else None))
            vfence()
            em.wait("tensor", em.tok("v"))
            for c in range(DC):
                slot = (c % 4) * 256
                t("transpose", pbT[:, slot:slot + 128],
                  ks[:, c * CH:(c + 1) * CH], cst["identb"], inc="t")
                em.wait("vector", em.tok("t"))
                v("tensor_copy", kst[:, c * CH:(c + 1) * CH],
                  pbT[:, slot:slot + 128], inc="v")
                em.wait("tensor", em.tok("v"))
            vfence()
            for pc in range(DC):
                csl = slice(pc * CH, (pc + 1) * CH)
                t("matmul", pb[2][:, 0:CH], ks[0:64, csl], qs[0:64, csl],
                  start=True, stop=True)
                t("matmul", pb[3][:, 0:CH], ks[64:128, csl], qs[64:128, csl],
                  start=True, stop=True, inc="t")
                em.wait("vector", em.tok("t"))
                v("tensor_mul", asb[:, 0:CH], pb[2][:, 0:CH], triu)
                v("tensor_mul", asb[:, CH:2 * CH], pb[3][:, 0:CH], triu, inc="v")
                vfence()
                em.wait("tensor", em.tok("v"))
                obank = pb[4 + pc // 4]
                osl = slice((pc % 4) * CH, (pc % 4) * CH + CH)
                vtb = ch * 1024 + pc * CH
                for hh in range(2):
                    b0 = 64 * hh
                    t("matmul", obank[b0:b0 + 64, osl],
                      vt_all[:, vtb + b0:vtb + b0 + 64],
                      asb[:, hh * CH:(hh + 1) * CH], start=True, stop=False,
                      tile_position=(0, b0))
                    t("matmul", obank[b0:b0 + 64, osl],
                      mstb[b0:b0 + 64, 64 * pc:64 * pc + 64],
                      qs[b0:b0 + 64, csl], start=False, stop=True,
                      tile_position=(b0, b0))
                    t("matmul", pb[6][b0:b0 + 64, 64 * pc:64 * pc + 64],
                      kst[:, pc * CH + b0:pc * CH + b0 + 64],
                      vt_all[:, vtb + b0:vtb + b0 + 64],
                      start=True, stop=True, tile_position=(0, b0),
                      inc=("t" if (pc == DC - 1 and hh == 1) else None))
            em.wait("vector", em.tok("t"))
            for c in range(DC):
                obank = pb[4 + c // 4]
                v("tensor_copy", ob[:, c * U + cb:c * U + cb + CH],
                  obank[:, (c % 4) * CH:(c % 4 + 1) * CH],
                  inc=("v" if c == DC - 1 else None))
            v("tensor_add", mtmp[:, 0:512], mst[:, 0:512], pb[6][:, 0:512])
            v("tensor_mul", mst[:, 0:512], mtmp[:, 0:512], plbc)
            v("tensor_copy", mstb[:, 0:512], mst[:, 0:512], inc="v")
            vfence()
            em.wait("tensor", em.tok("v"))

        ev2 = {}
        for oc in range(DC):
            bank = pb[4 + (oc % 2)]
            if oc >= 2:
                em.wait("tensor", ev2[oc - 2])
            for c in range(DC):
                t("matmul", bank[:, 0:U],
                  wo_sb[:, c * 1024 + oc * P:c * 1024 + (oc + 1) * P],
                  ob[:, c * U:(c + 1) * U], start=(c == 0), stop=(c == DC - 1),
                  inc=("t" if c == DC - 1 else None))
            em.wait("vector", em.tok("t"))
            v("tensor_scalar_add", yo[:, oc * U:(oc + 1) * U], bank[:, 0:U],
              bo_c[:, oc:oc + 1])
            v("tensor_add", yo[:, oc * U:(oc + 1) * U],
              yo[:, oc * U:(oc + 1) * U], xs[:, oc * U:(oc + 1) * U], inc="v")
            ev2[oc] = em.tok("v")
        chunks = [yo[:, c * U:(c + 1) * U] for c in range(DC)]
        sqs = [sq[:, c * U:(c + 1) * U] for c in range(DC)]
        em.wait("scalar", em.tok("v"))
        emit_ln_stats(em, pb, cst, chunks, sqs, rows, bc_mu, bc_r, U)
        for c in range(DC):
            sl = slice(c * U, (c + 1) * U)
            v("tensor_sub", yo[:, sl], yo[:, sl], bc_mu)
            v("tensor_mul", yo[:, sl], yo[:, sl], bc_r)
            v("tensor_scalar", yo[:, sl], yo[:, sl], lnw_c[:, c:c + 1],
              lnb_c[:, c:c + 1], ALU.mult, ALU.add,
              inc=("v" if c == DC - 1 else None))
        em.wait("scalar", em.tok("v"))
        emit_ln_stats(em, pb, cst, chunks, sqs, rows, bc_mu, bc_r, U)
        st_toks = []
        for c in range(DC):
            sl = slice(c * U, (c + 1) * U)
            v("tensor_sub", yo[:, sl], yo[:, sl], bc_mu)
            v("tensor_mul", yo[:, sl], yo[:, sl], bc_r, inc="v")
            em.wait("gpsimd", em.tok("v"))
            em.wait("gpsimd", g("tensor_add", yo[:, sl], yo[:, sl],
                                x2t[:, sl], inc="g"))
            st_toks.append(em.dma("gpsimd", xout[c * P:(c + 1) * P, t0:t0 + U],
                                  yo[:, sl]))
    ar.pop()
    return st_toks


def host_inputs_stage3(gg):
    h = {}
    h["WqT"] = np.ascontiguousarray(gg["L1_attn_Wq"].T)
    h["WkT"] = np.ascontiguousarray(gg["L1_attn_Wk"].T)
    h["WvT"] = np.ascontiguousarray(gg["L1_attn_Wv"].T)
    h["WoT"] = np.ascontiguousarray(gg["L1_attn_Wo"].T)
    h["betawT"] = np.ascontiguousarray(gg["L1_attn_beta_w"].T)
    h["rin_c"] = np.ascontiguousarray(gg["L1_attn_rin_w"].reshape(DC, P).T)
    h["rk_c"] = np.ascontiguousarray(gg["L1_attn_rk_w"].reshape(DC, P).T)
    h["rv_c"] = np.ascontiguousarray(gg["L1_attn_rv_w"].reshape(DC, P).T)
    h["bo_c"] = np.ascontiguousarray(gg["L1_attn_bo"].reshape(DC, P).T)
    h["lnw_c"] = np.ascontiguousarray(gg["L1_attn_ln_w"].reshape(DC, P).T)
    h["lnb_c"] = np.ascontiguousarray(gg["L1_attn_ln_b"].reshape(DC, P).T)
    h["betab_c"] = np.ascontiguousarray(gg["L1_attn_beta_b"].reshape(16, 1))
    h["basebeta_c"] = np.ascontiguousarray(
        gg["L1_attn_base_beta_logit"].reshape(16, 1))
    NH = 16
    esel = np.zeros((16, 1024), np.float32)
    for p in range(1024):
        esel[p // 64, p] = 1.0
    h["c_esel"] = esel
    epar = np.zeros((16, 128), np.float32)
    for hh in range(16):
        for p in range(128):
            if hh % 2 == p // 64:
                epar[hh, p] = 1.0
    h["c_epar"] = epar
    d64 = np.zeros((16, 512), np.float32)
    for hh in range(16):
        b = hh // 2
        d64[hh, 64 * b:64 * (b + 1)] = 1.0
    h["c_d64"] = d64
    h["c_triu"] = np.triu(np.ones((128, 128), np.float32))
    h["identb_d"] = np.eye(128, dtype=ml_dtypes.bfloat16)
    return h



def emit_stage4(em, ar, pb, pbT, cst, xin, xout, W, out_tok=None):
    TT = 256
    NT = T // TT
    NTC = TT // P
    v = lambda *a, **k: em.op("vector", *a, **k)
    def vw(*a, **k):
        em.wait("vector", v(*a, inc="v", **k))
    g = lambda *a, **k: em.op("gpsimd", *a, **k)
    s = lambda *a, **k: em.op("scalar", *a, **k)
    t = lambda *a, **k: em.op("tensor", *a, **k)

    ar.push()
    fb = ar.f32(8)

    def vfence():
        v("memset", fb[:, 0:1], 0.0, inc="v")
        em.wait("vector", em.tok("v"))

    def sfence():
        s("memzero", fb[:, 1:2], inc="s")
        em.wait("scalar", em.tok("s"))

    wi_sb = ar.bf16(DC * 4096)
    wo_sb = ar.bf16(32 * 1024)
    gate_sb = ar.bf16(DC * 8)
    lA_sb = ar.bf16(DC * 32)
    lB_sb = ar.bf16(4096)
    inter = ar.bf16(32 * TT)
    xa = ar.f32(TT * DC)
    sq = ar.f32(TT * DC)
    ya = ar.f32(TT * DC)
    xnb = ar.bf16(TT * DC)
    glog = ar.f32(NTC * 8)
    gmax = ar.f32(NTC)
    gm = ar.f32(NTC * 8)
    gcum = ar.f32(NTC * 8)
    sl_tm = ar.f32(NTC * 32)
    zb = ar.bf16(NTC * 128)
    zt = ar.bf16(TT)
    bc_mu = ar.f32(TT)
    bc_r = ar.f32(TT)
    rows = [ar.f32(TT) for _ in range(3)]
    cols = ar.f32(64)
    gbbc = ar.f32(8)
    wstg = ar.f32(2048)
    obuf = inter.bitcast(F32)[:, 0:2048]
    obk_tok = {}

    toks = [em.dma("sync", cols[:, 0:32], W["wib_c"]),
            em.dma("sync", cols[:, 32:40], W["wob_c"]),
            em.dma("sync", gbbc, W["gateb_bc"])]
    em.waits("vector", toks)
    em.waits("scalar", toks)
    for c in range(DC):
        for hh in range(2):
            em.wait("sync", em.tok("v"))
            tk = em.dma("sync", wstg,
                        W["wiT"][c * P:(c + 1) * P, hh * 2048:(hh + 1) * 2048])
            em.wait("vector", tk)
            v("tensor_copy",
              wi_sb[:, c * 4096 + hh * 2048:c * 4096 + (hh + 1) * 2048], wstg,
              inc="v")
    for kc in range(32):
        em.wait("sync", em.tok("v"))
        tk = em.dma("sync", wstg[:, 0:1024], W["woT"][kc * P:(kc + 1) * P, :])
        em.wait("vector", tk)
        v("tensor_copy", wo_sb[:, kc * 1024:(kc + 1) * 1024], wstg[:, 0:1024],
          inc="v")
    for c in range(DC):
        em.wait("sync", em.tok("v"))
        tk1 = em.dma("sync", wstg[:, 0:8], W["gateT"][c * P:(c + 1) * P, :])
        tk2 = em.dma("sync", wstg[:, 8:40], W["loraAT"][c * P:(c + 1) * P, :])
        em.waits("vector", [tk1, tk2])
        v("tensor_copy", gate_sb[:, c * 8:(c + 1) * 8], wstg[:, 0:8])
        v("tensor_copy", lA_sb[:, c * 32:(c + 1) * 32], wstg[:, 8:40], inc="v")
    for hh in range(2):
        em.wait("sync", em.tok("v"))
        tk = em.dma("sync", wstg[0:32, 0:2048],
                    W["loraBT"][:, hh * 2048:(hh + 1) * 2048])
        em.wait("vector", tk)
        v("tensor_copy", lB_sb[0:32, hh * 2048:(hh + 1) * 2048],
          wstg[0:32, 0:2048], inc="v")
    wtok = em.tok("v")

    st_toks = []
    for it in range(NT):
        t0 = it * TT
        em.wait("tensor", em.tok("v"))
        em.wait("tensor", em.tok("g"))
        em.wait("tensor", em.tok("s"))
        em.waits("vector", st_toks)
        em.waits("scalar", st_toks)
        em.wait("gpsimd", em.tok("v"))
        if it > 0:
            em.wait("sync", em.tok("g"))
            em.wait("sync", em.tok("v"))
        ld = [em.dma("sync", xa[:, c * TT:(c + 1) * TT],
                     xin[c * P:(c + 1) * P, t0:t0 + TT]) for c in range(DC)]
        em.waits("scalar", ld)
        em.waits("tensor", ld)
        em.waits("vector", ld)
        chunks = [xa[:, c * TT:(c + 1) * TT] for c in range(DC)]
        sqs = [sq[:, c * TT:(c + 1) * TT] for c in range(DC)]
        em.wait("scalar", em.tok("v"))
        emit_ln_stats(em, pb, cst, chunks, sqs, rows, bc_mu, bc_r, TT)
        for c in range(DC):
            sl = slice(c * TT, (c + 1) * TT)
            v("tensor_sub", sq[:, sl], xa[:, sl], bc_mu)
            v("tensor_mul", sq[:, sl], sq[:, sl], bc_r)
            v("tensor_copy", xnb[:, sl], sq[:, sl],
              inc=("v" if c == DC - 1 else None))
        vfence()
        xntok = em.tok("v")
        em.wait("tensor", xntok)
        for tc in range(NTC):
            for c in range(DC):
                t("matmul", pb[0][:, 0:8],
                  xnb[:, c * TT + tc * P:c * TT + (tc + 1) * P],
                  gate_sb[:, c * 8:(c + 1) * 8], start=(c == 0),
                  stop=(c == DC - 1))
            for c in range(DC):
                t("matmul", pb[1][:, 0:32],
                  xnb[:, c * TT + tc * P:c * TT + (tc + 1) * P],
                  lA_sb[:, c * 32:(c + 1) * 32], start=(c == 0),
                  stop=(c == DC - 1), inc=("t" if c == DC - 1 else None))
            em.wait("vector", em.tok("t"))
            gsl = slice(tc * 8, (tc + 1) * 8)
            vw("tensor_tensor", glog[:, gsl], pb[0][:, 0:8], gbbc, ALU.add)
            vw("tensor_reduce", gmax[:, tc:tc + 1], glog[:, gsl], AX, ALU.max)
            vw("tensor_scalar", gm[:, gsl], glog[:, gsl], gmax[:, tc:tc + 1],
               None, ALU.is_ge)
            vw("tensor_tensor_scan", gcum[:, gsl], gm[:, gsl],
               gm[:, gsl], 0.0, ALU.add, ALU.bypass)
            vw("tensor_single_scalar", gcum[:, gsl], gcum[:, gsl], 1.0,
               ALU.is_equal)
            vw("tensor_mul", gm[:, gsl], gm[:, gsl], gcum[:, gsl])
            vw("tensor_copy", sl_tm[:, tc * 32:(tc + 1) * 32], pb[1][:, 0:32])
            v("memset", zb[:, tc * 128:(tc + 1) * 128], 0.0)
            zblk = zb[:, tc * 128:tc * 128 + 32]
            z3 = zblk.rearrange("p (e r) -> p e r", r=4)
            for r in range(3):
                v("tensor_copy", z3[:, :, r:r + 1], gm[:, gsl])
            vw("tensor_copy", z3[:, :, 3:4], gm[:, gsl])
            vw("tensor_tensor", zblk, zblk, sl_tm[:, tc * 32:(tc + 1) * 32],
               ALU.mult)
            vfence()
            em.wait("tensor", em.tok("v"))
            t("transpose", pbT[:, 0:128], zb[:, tc * 128:(tc + 1) * 128],
              cst["identb"], inc="t")
            em.wait("vector", em.tok("t"))
            v("tensor_copy", zt[0:32, tc * P:(tc + 1) * P], pbT[0:32, 0:128],
              inc="v")
            vfence()
        em.wait("tensor", em.tok("v"))
        ev = {}
        for fc in range(32):
            bank = pb[4 + (fc % 2)]
            if fc >= 2:
                em.wait("tensor", ev[fc - 2])
            for c in range(DC):
                t("matmul", bank[:, 0:TT],
                  wi_sb[:, c * 4096 + fc * P:c * 4096 + (fc + 1) * P],
                  xnb[:, c * TT:(c + 1) * TT], start=(c == 0), stop=False)
            t("matmul", bank[:, 0:TT], lB_sb[0:32, fc * P:(fc + 1) * P],
              zt[0:32, 0:TT], start=False, stop=True,
              inc="t")
            em.wait("scalar", em.tok("t"))
            em.wait("scalar", em.tok("v"))
            s("activation", inter[:, fc * TT:(fc + 1) * TT], bank[:, 0:TT],
              AF.Relu, bias=cols[:, fc:fc + 1], inc="s")
            ev[fc] = em.tok("s")
        sfence()
        em.wait("tensor", em.tok("s"))
        ev2 = {}
        for oc in range(DC):
            bank = pb[4 + (oc % 2)]
            if oc >= 2:
                em.wait("tensor", ev2[oc - 2])
            for kc in range(32):
                t("matmul", bank[:, 0:TT],
                  wo_sb[:, kc * 1024 + oc * P:kc * 1024 + (oc + 1) * P],
                  inter[:, kc * TT:(kc + 1) * TT], start=(kc == 0),
                  stop=(kc == 31), inc=("t" if kc == 31 else None))
            em.wait("vector", em.tok("t"))
            v("tensor_scalar_add", ya[:, oc * TT:(oc + 1) * TT], bank[:, 0:TT],
              cols[:, 32 + oc:33 + oc], inc="v")
            ev2[oc] = em.tok("v")
        if em.dbg is not None and it == 0:
            em.wait("gpsimd", em.tok("v"))
            ytk = []
            for c in range(DC):
                ytk.append(em.dma("gpsimd",
                                  em.dbg["y"][c * P:(c + 1) * P, 0:TT],
                                  ya[:, c * TT:(c + 1) * TT]))
            em.waits("vector", ytk)
            em.dma("gpsimd", em.dbg["S"][0:P, 0:16],
                   glog[:, 0:16])
            em.dma("gpsimd", em.dbg["S"][0:P, 16:32], gm[:, 0:16])
            em.dma("gpsimd", em.dbg["k"][0:P, 0:256], zb[:, 0:256])
            em.dma("gpsimd", em.dbg["k"][128:160, 256:512], zt[0:32, 0:256])
            for kc in range(8):
                em.dma("gpsimd", em.dbg["Pf"][kc * P:(kc + 1) * P, 0:TT],
                       inter.bitcast(F32)[:, kc * TT // 2:(kc + 1) * TT // 2]
                       if False else inter[:, kc * TT:(kc + 1) * TT])
        chunks = [ya[:, c * TT:(c + 1) * TT] for c in range(DC)]
        em.wait("scalar", em.tok("v"))
        emit_ln_stats(em, pb, cst, chunks, sqs, rows, bc_mu, bc_r, TT)
        st_toks = []
        if out_tok is None:
            for c in range(DC):
                sl = slice(c * TT, (c + 1) * TT)
                v("tensor_sub", ya[:, sl], ya[:, sl], bc_mu)
                v("tensor_mul", ya[:, sl], ya[:, sl], bc_r, inc="v")
                em.wait("gpsimd", em.tok("v"))
                em.wait("gpsimd", g("tensor_add", ya[:, sl], ya[:, sl],
                                    xa[:, sl], inc="g"))
                st_toks.append(em.dma("gpsimd",
                                      xout[c * P:(c + 1) * P, t0:t0 + TT],
                                      ya[:, sl]))
        else:
            for c in range(DC):
                sl = slice(c * TT, (c + 1) * TT)
                v("tensor_sub", ya[:, sl], ya[:, sl], bc_mu)
                v("tensor_mul", ya[:, sl], ya[:, sl], bc_r)
                v("tensor_add", ya[:, sl], ya[:, sl], xa[:, sl],
                  inc=("v" if c == DC - 1 else None))
            vfence()
            if em.dbg is not None and it == 0:
                em.wait("gpsimd", em.tok("v"))
                dtk = []
                for c in range(DC):
                    dtk.append(em.dma("gpsimd",
                                      em.dbg["cs"][c * P:(c + 1) * P, 0:TT],
                                      ya[:, c * TT:(c + 1) * TT]))
                em.waits("vector", dtk)
                em.waits("tensor", dtk)
            em.wait("tensor", em.tok("v"))
            for tc in range(NTC):
                for c in range(DC):
                    bankT = pb[4 + (c % 2)]
                    prev = (tc, c - 2) if c >= 2 else (tc - 1, DC - 2)
                    em.wait("tensor", obk_tok.get(prev))
                    t("transpose", bankT[:, 0:P],
                      ya[:, c * TT + tc * P:c * TT + (tc + 1) * P],
                      cst["ident_f32"], inc="t")
                    em.wait("vector", em.tok("t"))
                    v("tensor_copy", obuf[:, tc * 1024 + c * P:
                                          tc * 1024 + (c + 1) * P],
                      bankT[:, 0:P], inc="v")
                    obk_tok[(tc, c)] = em.tok("v")
                em.wait("gpsimd", em.tok("v"))
                if em.dbg is not None and it == 0:
                    em.dma("gpsimd", em.dbg["dt"][tc * P:(tc + 1) * P, 0:512],
                           obuf[:, tc * 1024:tc * 1024 + 512])
                st_toks.append(em.dma("gpsimd",
                                      out_tok[t0 + tc * P:t0 + (tc + 1) * P, :],
                                      obuf[:, tc * 1024:(tc + 1) * 1024]))
            em.waits("vector", st_toks)
    ar.pop()
    return st_toks


def host_inputs_stage4(gg):
    h = {}
    h["gateT"] = np.ascontiguousarray(gg["L1_ffn_gate_W"].T)
    h["wiT"] = np.ascontiguousarray(gg["L1_ffn_wi_W"].T)
    h["woT"] = np.ascontiguousarray(gg["L1_ffn_wo_W"].T)
    h["loraAT"] = np.ascontiguousarray(
        gg["L1_ffn_lora_A"].transpose(2, 0, 1).reshape(1024, 32))
    h["loraBT"] = np.ascontiguousarray(
        gg["L1_ffn_lora_B"].transpose(0, 2, 1).reshape(32, 4096))
    h["gateb_bc"] = np.ascontiguousarray(
        np.tile(gg["L1_ffn_gate_b"][None, :], (P, 1)))
    h["wib_c"] = np.ascontiguousarray(gg["L1_ffn_wi_b"].reshape(32, P).T)
    h["wob_c"] = np.ascontiguousarray(gg["L1_ffn_wo_b"].reshape(DC, P).T)
    h["identb_d"] = np.eye(128, dtype=ml_dtypes.bfloat16)
    return h



def emit_final_transpose(em, ar, pb, cst, xin, out, intoks):
    ar.push()
    buf = ar.f32(1024)
    tp = ar.f32(1024)
    idf = cst["ident_f32"]
    em.waits("sync", intoks)
    sttok = None
    for tc in range(T // P):
        prev = []
        for c in range(DC):
            if sttok is not None and tc > 0:
                pass
            tk = em.dma("sync", buf[:, c * P:(c + 1) * P],
                        xin[c * P:(c + 1) * P, tc * P:(tc + 1) * P])
            em.wait("tensor", tk)
            em.wait("tensor", em.tok("v"))
            em.op("tensor", "transpose", pb[4 + (c % 2)][:, 0:P],
                  buf[:, c * P:(c + 1) * P], idf, inc="t")
            em.wait("vector", em.tok("t"))
            if sttok is not None:
                em.wait("vector", sttok)
            em.op("vector", "tensor_copy", tp[:, c * P:(c + 1) * P],
                  pb[4 + (c % 2)][:, 0:P], inc="v")
        em.wait("gpsimd", em.tok("v"))
        sttok = em.dma("gpsimd", out[tc * P:(tc + 1) * P, :], tp)
        em.wait("sync", em.tok("v"))
    em.wait("gpsimd", sttok)
    ar.pop()



def build_program(stages=(1,), debug=False):
    nc = bass.Bass("TRN2", target_bir_lowering=False, debug=False)
    es = ExitStack()
    xT = nc.dram_tensor("xT", [D, T], F32, kind="ExternalInput").ap()
    out = nc.dram_tensor("out", [T, D], F32, kind="ExternalOutput").ap()
    xd = [nc.dram_tensor(f"xd{i}", [D, T], F32, kind="Internal").ap()
          for i in range(2)]

    W = {}
    def win(name, shape, dt=F32):
        W[name] = nc.dram_tensor(name, list(shape), dt, kind="ExternalInput").ap()

    win("identf", (P, P))
    if 1 in stages:
        win("WxT", (1024, 2048)); win("WdtT", (1024, 1024))
        win("Alog_c", (128, 8)); win("bx_cols_d", (128, 16))
        win("bdt_cols_d", (128, 8))
    if 2 in stages:
        win("w1T", (1024, 4096)); win("w2T", (1024, 4096))
        win("h1T", (4096, 4096)); win("h2T", (4096, 4096))
        win("w3T", (4096, 1024))
        for n in ("w1s_c", "w2s_c", "h1s_c", "h2s_c"):
            win(n, (128, 32))
        win("w3s_c", (128, 8))
    if 3 in stages:
        for n in ("WqT", "WkT", "WvT", "WoT"):
            win(n, (1024, 1024))
        win("betawT", (1024, 16))
        for n in ("rin_c", "rk_c", "rv_c", "bo_c", "lnw_c", "lnb_c"):
            win(n, (128, 8))
        win("betab_c", (16, 1)); win("basebeta_c", (16, 1))
        win("c_esel", (16, 1024)); win("c_epar", (16, 128))
        win("c_d64", (16, 512)); win("c_triu", (128, 128))
    if 3 in stages or 4 in stages:
        win("identb_d", (128, 128), BF16)
    if 4 in stages:
        win("gateT", (1024, 8)); win("wiT", (1024, 4096))
        win("woT", (4096, 1024))
        win("loraAT", (1024, 32)); win("loraBT", (32, 4096))
        win("gateb_bc", (128, 8)); win("wib_c", (128, 32))
        win("wob_c", (128, 8))

    with es:
        ar = Arena(nc, es, 51500)
        pb = [es.enter_context(nc.psum_tensor(f"pb{i}", [P, 512], F32))
              for i in range(7)]
        pbT = es.enter_context(nc.psum_tensor("pbT", [P, 1024], BF16))
        em = EM(nc, es)
        em.dbg = None
        if debug:
            em.dbg = {}
            for nm in ("dt", "b", "tc", "S", "Pf", "cs", "y", "psum0", "rsx",
                       "sbc", "colw", "wq0"):
                em.dbg[nm] = nc.dram_tensor("dbg_" + nm, [D, 512], F32,
                                            kind="ExternalOutput").ap()
            em.dbg["k"] = nc.dram_tensor("dbg_k", [D, 512], BF16,
                                         kind="ExternalOutput").ap()

        cst = {}
        cwt = ar.f32(4)
        orow = ar.f32(128)
        idf = ar.f32(128)
        cst["ones_c"] = cwt[:, 0:1]
        cst["ones_r"] = orow[0:1, 0:128]
        cst["ident_f32"] = idf
        em.op("vector", "memset", cwt[:, 0:1], 1.0)
        em.op("vector", "memset", orow[0:1, 0:128], 1.0, inc="v")
        tki = em.dma("sync", idf, W["identf"])
        em.wait("tensor", tki)
        if "identb_d" in W:
            idb = ar.bf16(128)
            cst["identb"] = idb
            tki = em.dma("sync", idb, W["identb_d"])
            em.wait("tensor", tki)

        if 1 in stages:
            bxc = ar.f32(16)
            bdtc = ar.f32(8)
            t1 = em.dma("sync", bxc, W["bx_cols_d"])
            t2 = em.dma("sync", bdtc, W["bdt_cols_d"])
            em.waits("gpsimd", [t1, t2])
            em.waits("scalar", [t1, t2])
            em.waits("vector", [t1, t2])
            W["bx_cols"] = bxc
            W["bdt_cols"] = bdtc

        cur = xT
        toks = []
        if 2 in stages:
            kind2 = "ExternalOutput" if debug else "Internal"
            ud = nc.dram_tensor("ud", [4096, T], BF16, kind=kind2).ap()
            hmd = nc.dram_tensor("hmd", [4096, T], BF16, kind=kind2).ap()
            hfd = nc.dram_tensor("hfd", [4096, T], BF16, kind=kind2).ap()
        if 1 in stages:
            toks = emit_stage1(em, ar, pb, cst, cur, xd[0], W)
            cur = xd[0]
        if 2 in stages:
            em.waits("sync", toks)
            toks = emit_stage2(em, ar, pb, cst, cur, xd[1], W, ud, hmd, hfd)
            cur = xd[1]
        if 3 in stages:
            em.waits("sync", toks)
            dst = xd[0] if cur is not xd[0] else xd[1]
            toks = emit_stage3(em, ar, pb, pbT, cst, cur, dst, W)
            cur = dst
        if 4 in stages:
            em.waits("sync", toks)
            dst = xd[0] if cur is not xd[0] else xd[1]
            toks = emit_stage4(em, ar, pb, pbT, cst, cur, dst, W, out_tok=out)
            em.waits("gpsimd", toks)
        else:
            emit_final_transpose(em, ar, pb, cst, cur, out, toks)

        with nc.Block() as block:
            em.assemble(block)
    return nc


def host_inputs_stage1(g):
    h = {}
    h["WxT"] = np.ascontiguousarray(g["L0_attn_W_x"].T)
    h["WdtT"] = np.ascontiguousarray(g["L0_attn_W_dt"].T)
    h["Alog_c"] = np.ascontiguousarray(g["L0_attn_A_log"].reshape(DC, P).T)
    h["bx_cols_d"] = np.ascontiguousarray(g["L0_attn_b_x"].reshape(16, P).T)
    h["bdt_cols_d"] = np.ascontiguousarray(g["L0_attn_b_dt"].reshape(DC, P).T)
    return h


_NC_CACHE = {}


def kernel(**inputs):
    if "prog" not in _NC_CACHE:
        _NC_CACHE["prog"] = build_program((1, 2, 3, 4))
    nc = _NC_CACHE["prog"]
    h = {"identf": np.eye(P, dtype=np.float32)}
    h.update(host_inputs_stage1(inputs))
    h.update(host_inputs_stage2(inputs))
    h.update(host_inputs_stage3(inputs))
    h.update(host_inputs_stage4(inputs))
    x = np.asarray(inputs["L0_attn_x"], dtype=np.float32)
    in_maps = []
    for b in range(8):
        m = dict(h)
        m["xT"] = np.ascontiguousarray(x[b].T)
        in_maps.append(m)
    res = run_bass_kernel_spmd(nc, in_maps, core_ids=list(range(8)))
    return np.stack([np.asarray(res.results[b]["out"]) for b in range(8)])


# version 06
# speedup vs baseline: 1.0348x; 1.0348x over previous
import numpy as np
import ml_dtypes
from contextlib import ExitStack
import concourse.bass as bass
import concourse.mybir as mybir
from concourse.bass_utils import run_bass_kernel_spmd

F32 = mybir.dt.float32
BF16 = mybir.dt.bfloat16
I32 = mybir.dt.int32
AF = mybir.ActivationFunctionType
ALU = mybir.AluOpType
AX = mybir.AxisListType.X

P = 128
D = 1024
T = 4096
DC = D // P
FLT_MIN = 1.17549435e-38
FLT_MAX = 3.4028234663852886e38

ENGINES = ("sync", "vector", "scalar", "tensor", "gpsimd")


class EM:

    def __init__(self, nc, es):
        self.nc = nc
        self.es = es
        self.prog = {e: [] for e in ENGINES}
        self.sems = {}
        self.cnt = {}
        self.dbg = None

    def sem(self, name):
        if name not in self.sems:
            self.sems[name] = self.es.enter_context(self.nc.semaphore(name))
            self.cnt[name] = 0
        return self.sems[name]

    def op(self, eng, method, *args, inc=None, **kw):
        s = None
        if inc:
            s = self.sem(inc)
            self.cnt[inc] += 1

        def fn(e, method=method, args=args, kw=kw, s=s):
            ins = getattr(e, method)(*args, **kw)
            if s is not None:
                ins.then_inc(s, 1)
        self.prog[eng].append(fn)
        return (inc, self.cnt[inc]) if inc else None

    def dma(self, q, out, in_):
        name = "dq_" + q
        s = self.sem(name)
        self.cnt[name] += 16
        tgt = self.cnt[name]

        def fn(e, out=out, in_=in_, s=s):
            e.dma_start(out=out, in_=in_).then_inc(s, 16)
        self.prog[q].append(fn)
        return (name, tgt)

    def wait(self, eng, tok):
        if tok is None:
            return
        name, val = tok
        if val <= 0:
            return
        s = self.sem(name)
        self.prog[eng].append(lambda e, s=s, v=val: e.wait_ge(s, v))

    def waits(self, eng, toks):
        best = {}
        for t in toks:
            if t is None:
                continue
            n, v = t
            best[n] = max(best.get(n, 0), v)
        for n, v in best.items():
            self.wait(eng, (n, v))

    def tok(self, name):
        return (name, self.cnt.get(name, 0))

    def assemble(self, block):
        for engname in ENGINES:
            body = self.prog[engname]
            if not body:
                continue

            def runner(e, body=body):
                for fn in body:
                    fn(e)
            getattr(block, engname)(runner)


class Arena:

    def __init__(self, nc, es, ncols_f32, name="arena"):
        self.h = es.enter_context(nc.sbuf_tensor(name, [P, ncols_f32], F32))
        self.hb = self.h.bitcast(BF16)
        self.hi = self.h.bitcast(I32)
        self.ncols = ncols_f32
        self.off = 0
        self.marks = []

    def _take(self, n):
        assert self.off + n <= self.ncols, \
            f"SBUF arena overflow: {self.off}+{n} > {self.ncols}"
        o = self.off
        self.off += n
        return o

    def f32(self, ncols):
        o = self._take(ncols)
        return self.h[:, o:o + ncols]

    def bf16(self, ncols):
        o = self._take((ncols + 1) // 2)
        return self.hb[:, 2 * o:2 * o + ncols]

    def i32(self, ncols):
        o = self._take(ncols)
        return self.hi[:, o:o + ncols]

    def push(self):
        self.marks.append(self.off)

    def pop(self):
        self.off = self.marks.pop()



def emit_stage1(em, ar, pb, cst, xin, xout, W):
    nc = em.nc
    TT = 512
    NTILE = T // TT

    ar.push()
    wqx = ar.bf16(16384)
    wqdt = ar.bf16(8192)
    xa = ar.f32(TT * DC)
    xna = ar.f32(TT * DC)
    ka = ar.bf16(TT * DC)
    ba = ar.f32(TT * DC)
    ca = ar.f32(TT * DC)
    ya = ar.f32(TT * DC)
    dta = ar.f32(TT * DC)
    bc_mu = ar.f32(TT)
    bc_r = ar.f32(TT)
    bc_s = ar.f32(TT)
    bc_rsx = ar.f32(TT)
    bc_rsdt = ar.f32(TT)
    w_s = ar.f32(TT)
    w_p = ar.f32(TT)
    w_pf = ar.f32(TT)
    w_psh = ar.f32(TT)
    w_rp = ar.f32(TT)
    w_g = ar.f32(TT)
    w_cs = ar.f32(TT)
    w_t0 = ar.f32(TT)
    w_t1 = ar.f32(TT)
    w_m = ar.i32(TT)
    zeros = ar.f32(TT)
    carry_S = ar.f32(DC)
    carry_PF = ar.f32(DC)
    carry_CS = ar.f32(DC)
    acc_abs = ar.f32(2)
    colw = ar.f32(8)
    a_col = ar.f32(DC)
    a_tmp = ar.f32(4 * DC)
    rowbuf = ar.f32(8)

    ones_c = cst["ones_c"]
    ones_r = cst["ones_r"]
    v = lambda *a, **k: em.op("vector", *a, **k)
    g = lambda *a, **k: em.op("gpsimd", *a, **k)
    s = lambda *a, **k: em.op("scalar", *a, **k)
    t = lambda *a, **k: em.op("tensor", *a, **k)

    v("memset", zeros, 0.0, inc="v")
    tk = em.dma("sync", a_tmp[:, 0:DC], W["Alog_c"])
    em.wait("scalar", tk)
    em.wait("scalar", s("activation", a_tmp[:, DC:2 * DC], a_tmp[:, 0:DC], AF.Abs,
                        inc="s"))
    em.wait("scalar", s("activation", a_tmp[:, 2 * DC:3 * DC], a_tmp[:, DC:2 * DC],
                        AF.Exp, scale=-1.0, inc="s"))
    s("activation", a_tmp[:, 3 * DC:4 * DC], a_tmp[:, 2 * DC:3 * DC], AF.Ln,
      bias=1.0, inc="s")
    em.wait("vector", em.tok("s"))
    em.wait("vector", v("tensor_relu", a_col, a_tmp[:, 0:DC], inc="v"))
    em.wait("vector", v("tensor_add", a_col, a_col, a_tmp[:, 3 * DC:4 * DC],
                        inc="v"))
    v("tensor_scalar_mul", a_col, a_col, -1.0, inc="v")

    def quant_weight(wT_dram, n_out, dst_bf16, acc_slot, inv_count, ws_slot):
        v("memset", acc_abs[:, acc_slot:acc_slot + 1], 0.0, inc="v")
        cons = {}
        for c in range(DC):
            par = c % 2
            stg = xna[:, par * n_out:(par + 1) * n_out]
            em.wait("sync", cons.get(c - 2, em.tok("v") if c < 2 else None))
            tk = em.dma("sync", stg, wT_dram[c * P:(c + 1) * P, :])
            em.wait("vector", tk)
            em.wait("vector", v("tensor_reduce", w_t0[:, 0:1], stg,
                                AX, ALU.add, apply_absolute_value=True, inc="v"))
            em.wait("vector", v("tensor_add", acc_abs[:, acc_slot:acc_slot + 1],
                                acc_abs[:, acc_slot:acc_slot + 1], w_t0[:, 0:1],
                                inc="v"))
            cons[c] = em.tok("v")
        em.wait("tensor", em.tok("v"))
        t("matmul", pb[0][0:1, 0:1], ones_c, acc_abs[:, acc_slot:acc_slot + 1],
          start=True, stop=True, inc="t")
        em.wait("vector", em.tok("t"))
        em.wait("vector", v("tensor_scalar", rowbuf[0:1, 0:1], pb[0][0:1, 0:1],
                            inv_count, 1e-5, ALU.mult, ALU.add, inc="v"))
        em.wait("vector", v("reciprocal", rowbuf[0:1, 1:2], rowbuf[0:1, 0:1],
                            inc="v"))
        v("tensor_copy", rowbuf[0:1, 2:3], rowbuf[0:1, 0:1], inc="v")
        em.wait("tensor", em.tok("v"))
        t("matmul", pb[1][:, 0:2], ones_r, rowbuf[0:1, 1:3], start=True, stop=True,
          inc="t")
        em.wait("vector", em.tok("t"))
        em.wait("vector", v("tensor_copy", colw[:, ws_slot:ws_slot + 2],
                            pb[1][:, 0:2], inc="v"))
        cons = {}
        for c in range(DC):
            par = c % 2
            stg = xna[:, par * n_out:(par + 1) * n_out]
            em.wait("sync", cons.get(c - 2, em.tok("v") if c < 2 else None))
            tk = em.dma("sync", stg, wT_dram[c * P:(c + 1) * P, :])
            em.wait("vector", tk)
            v("tensor_scalar_mul", ba[:, 0:n_out], stg,
              colw[:, ws_slot:ws_slot + 1])
            v("tensor_scalar", ba[:, 0:n_out], ba[:, 0:n_out], -1.0, 1.0,
              ALU.max, ALU.min)
            v("tensor_copy", ca.bitcast(I32)[:, 0:n_out], ba[:, 0:n_out])
            v("tensor_copy", dst_bf16[:, c * n_out:(c + 1) * n_out],
              ca.bitcast(I32)[:, 0:n_out], inc="v")
            cons[c] = em.tok("v")

    quant_weight(W["WxT"], 2048, wqx, 0, 1.0 / (2048 * 1024), 0)
    quant_weight(W["WdtT"], 1024, wqdt, 1, 1.0 / (1024 * 1024), 2)
    v("memset", carry_S, 0.0)
    v("memset", carry_PF, 1.0)
    v("memset", carry_CS, 0.0, inc="v")

    st_toks = []
    for it in range(NTILE):
        t0 = it * TT
        em.wait("tensor", em.tok("v"))
        em.wait("tensor", em.tok("g"))
        em.wait("tensor", em.tok("s"))
        em.wait("gpsimd", em.tok("v"))
        em.waits("scalar", st_toks)
        em.waits("vector", st_toks)
        if it > 0:
            em.wait("sync", em.tok("g"))
        ldtoks = []
        for c in range(DC):
            ldtoks.append(em.dma("sync", xa[:, c * TT:(c + 1) * TT],
                                 xin[c * P:(c + 1) * P, t0:t0 + TT]))
        em.waits("scalar", ldtoks)
        em.waits("tensor", ldtoks + [em.tok("v")])
        for c in range(DC):
            s("activation", ya[:, c * TT:(c + 1) * TT], xa[:, c * TT:(c + 1) * TT],
              AF.Square, inc=("s" if c == DC - 1 else None))
        for c in range(DC):
            t("matmul", pb[0][0:1, 0:TT], ones_c, xa[:, c * TT:(c + 1) * TT],
              start=(c == 0), stop=(c == DC - 1))
        em.wait("tensor", em.tok("s"))
        for c in range(DC):
            t("matmul", pb[1][0:1, 0:TT], ones_c, ya[:, c * TT:(c + 1) * TT],
              start=(c == 0), stop=(c == DC - 1), inc=("t" if c == DC - 1 else None))
        em.wait("vector", em.tok("t"))
        v("tensor_scalar_mul", w_t0[0:1, :], pb[0][0:1, 0:TT], 1.0 / D)
        v("tensor_scalar_mul", w_t1[0:1, :], pb[1][0:1, 0:TT], 1.0 / D)
        v("tensor_mul", w_g[0:1, :], w_t0[0:1, :], w_t0[0:1, :])
        v("tensor_sub", w_t1[0:1, :], w_t1[0:1, :], w_g[0:1, :])
        v("tensor_scalar_add", w_t1[0:1, :], w_t1[0:1, :], 1e-5, inc="v")
        em.wait("scalar", em.tok("v"))
        s("activation", w_cs[0:1, :], w_t1[0:1, :], AF.Sqrt, inc="s")
        em.wait("vector", em.tok("s"))
        v("reciprocal", w_t1[0:1, :], w_cs[0:1, :], inc="v")
        em.wait("tensor", em.tok("v"))
        t("matmul", pb[2][:, 0:TT], ones_r, w_t0[0:1, :], start=True, stop=True)
        t("matmul", pb[3][:, 0:TT], ones_r, w_t1[0:1, :], start=True, stop=True,
          inc="t")
        em.wait("vector", em.tok("t"))
        v("tensor_copy", bc_mu, pb[2][:, 0:TT])
        v("tensor_copy", bc_r, pb[3][:, 0:TT])
        for c in range(DC):
            sl = slice(c * TT, (c + 1) * TT)
            v("tensor_sub", xna[:, sl], xa[:, sl], bc_mu)
            v("tensor_mul", xna[:, sl], xna[:, sl], bc_r,
              inc=("v" if c == DC - 1 else None))
        em.wait("scalar", em.tok("v"))
        for c in range(DC):
            s("activation", ya[:, c * TT:(c + 1) * TT], xna[:, c * TT:(c + 1) * TT],
              AF.Abs, inc=("s" if c == DC - 1 else None))
        em.wait("vector", em.tok("s"))
        for c in range(1, DC):
            v("tensor_tensor", ya[:, 0:TT], ya[:, 0:TT], ya[:, c * TT:(c + 1) * TT],
              ALU.max)
        v("tensor_copy", w_t0[0:64, :], ya[64:128, 0:TT])
        v("tensor_tensor", ya[0:64, 0:TT], ya[0:64, 0:TT], w_t0[0:64, :], ALU.max)
        v("tensor_copy", w_t0[0:32, :], ya[32:64, 0:TT])
        v("tensor_tensor", ya[0:32, 0:TT], ya[0:32, 0:TT], w_t0[0:32, :], ALU.max)
        for k in (16, 8, 4, 2, 1):
            mask = [(i + k) % 32 for i in range(32)]
            v("stream_shuffle", ya[0:32, TT:2 * TT], ya[0:32, 0:TT], mask)
            v("tensor_tensor", ya[0:32, 0:TT], ya[0:32, 0:TT], ya[0:32, TT:2 * TT],
              ALU.max)
        v("tensor_scalar_max", ya[0:32, 0:TT], ya[0:32, 0:TT], 1e-5)
        v("reciprocal", ya[0:32, TT:2 * TT], ya[0:32, 0:TT])
        v("tensor_scalar_mul", bc_s[0:32, :], ya[0:32, TT:2 * TT], 127.0)
        for pbase in (32, 64, 96):
            v("tensor_copy", bc_s[pbase:pbase + 32, :], bc_s[0:32, :])
        v("reciprocal", bc_rsx, bc_s)
        v("tensor_scalar_mul", bc_rsdt, bc_rsx, colw[:, 3:4])
        v("tensor_scalar_mul", bc_rsx, bc_rsx, colw[:, 1:2])
        for c in range(DC):
            sl = slice(c * TT, (c + 1) * TT)
            v("tensor_mul", xna[:, sl], xna[:, sl], bc_s)
            v("tensor_copy", ya.bitcast(I32)[:, sl], xna[:, sl])
            v("tensor_copy", ka[:, sl], ya.bitcast(I32)[:, sl],
              inc=("v" if c == DC - 1 else None))
        ktok = em.tok("v")

        em.wait("tensor", ktok)
        evac_done = {}
        order = list(range(8, 16)) + list(range(16, 24)) + list(range(0, 8))
        for j2, oc in enumerate(order):
            bank = pb[4 + (j2 % 2)]
            if j2 >= 2:
                em.wait("tensor", evac_done[j2 - 2])
            for c in range(DC):
                if oc < 8:
                    stat = wqdt[:, c * 1024 + oc * P:c * 1024 + (oc + 1) * P]
                else:
                    o2 = oc - 8
                    stat = wqx[:, c * 2048 + o2 * P:c * 2048 + (o2 + 1) * P]
                t("matmul", bank[:, 0:TT], stat, ka[:, c * TT:(c + 1) * TT],
                  start=(c == 0), stop=(c == DC - 1),
                  inc=("t" if c == DC - 1 else None))
            em.wait("vector", em.tok("t"))
            ch = oc % 8
            sl = slice(ch * TT, (ch + 1) * TT)
            dst = dta if oc < 8 else (ba if oc < 16 else ca)
            scale = bc_rsdt if oc < 8 else bc_rsx
            v("tensor_tensor", dst[:, sl], bank[:, 0:TT], scale, ALU.mult,
              inc="v")
            evac_done[j2] = em.tok("v")

        em.wait("gpsimd", em.tok("v"))
        for ch in range(DC):
            sl = slice(ch * TT, (ch + 1) * TT)
            g("tensor_scalar_add", dta[:, sl], dta[:, sl],
              W["bdt_cols"][:, ch:ch + 1])
            g("tensor_scalar_add", ba[:, sl], ba[:, sl],
              W["bx_cols"][:, ch:ch + 1])
            g("tensor_scalar_add", ca[:, sl], ca[:, sl],
              W["bx_cols"][:, ch + 8:ch + 9],
              inc=("g" if ch == DC - 1 else None))
        em.wait("vector", em.tok("g"))
        v("tensor_scalar", ba[:, :], ba[:, :], -5.0, 5.0, ALU.max, ALU.min)
        v("tensor_scalar", ca[:, :], ca[:, :], -5.0, 5.0, ALU.max, ALU.min,
          inc="v")
        em.wait("scalar", em.tok("g"))
        em.wait("scalar", em.tok("v"))
        s("activation", ca[:, :], ca[:, :], AF.Tanh)
        s("activation", ya[:, :], dta[:, :], AF.Abs, inc="s")
        s("activation", ya[:, :], ya[:, :], AF.Exp, scale=-1.0)
        s("activation", ya[:, :], ya[:, :], AF.Ln, bias=1.0, inc="s")
        em.wait("vector", em.tok("s"))
        v("scalar_tensor_tensor", dta[:, :], dta[:, :], 0.0, ya[:, :],
          ALU.max, ALU.add)
        v("tensor_scalar", dta[:, :], dta[:, :], 0.001, 1.0, ALU.max, ALU.min)
        v("tensor_mul", ba[:, :], dta[:, :], ba[:, :])
        for ch in range(DC):
            sl = slice(ch * TT, (ch + 1) * TT)
            v("tensor_scalar", dta[:, sl], dta[:, sl], a_col[:, ch:ch + 1],
              -10.0, ALU.mult, ALU.max)
        v("tensor_scalar_min", dta[:, :], dta[:, :], 0.0, inc="v")

        if em.dbg is not None and it == 0:
            em.wait("gpsimd", em.tok("v"))
            em.wait("gpsimd", em.tok("s"))
            dtk = []
            for nm, buf in [("dt", dta), ("b", ba), ("tc", ca)]:
                for c in range(DC):
                    dtk.append(em.dma("gpsimd",
                                      em.dbg[nm][c * P:(c + 1) * P, 0:TT],
                                      buf[:, c * TT:(c + 1) * TT]))
            for c in range(DC):
                dtk.append(em.dma("gpsimd", em.dbg["k"][c * P:(c + 1) * P, 0:TT],
                                  ka[:, c * TT:(c + 1) * TT]))
            em.waits("vector", dtk)
        em.wait("vector", em.tok("s"))
        for c in range(DC):
            sl = slice(c * TT, (c + 1) * TT)
            cc = slice(c, c + 1)
            em.wait("vector", v("tensor_tensor_scan", ya[:, sl], dta[:, sl],
                                zeros, carry_S[:, cc], ALU.add, ALU.add,
                                inc="v"))
            v("tensor_copy", carry_S[:, cc], ya[:, c * TT + TT - 1:c * TT + TT],
              inc=("v" if c == DC - 1 else None))
        em.wait("scalar", em.tok("v"))
        s("activation", dta[:, :], ya[:, :], AF.Exp, inc="s")
        em.wait("vector", em.tok("s"))
        for c in range(DC):
            sl = slice(c * TT, (c + 1) * TT)
            cc = slice(c, c + 1)
            v("tensor_single_scalar", w_m, dta[:, sl], FLT_MIN, ALU.is_ge)
            v("select", w_pf, w_m, dta[:, sl], zeros)
            v("tensor_copy", w_psh[:, 0:1], carry_PF[:, cc])
            em.wait("vector", v("tensor_copy", w_psh[:, 1:TT], w_pf[:, 0:TT - 1],
                                inc="v"))
            em.wait("vector", v("tensor_copy", carry_PF[:, cc],
                                w_pf[:, TT - 1:TT], inc="v"))
            v("reciprocal", w_rp, w_psh)
            v("tensor_mul", w_g, ba[:, sl], w_rp)
            em.wait("vector", v("tensor_tensor_scan", w_cs, w_g, zeros,
                                carry_CS[:, cc], ALU.add, ALU.add, inc="v"))
            v("tensor_copy", carry_CS[:, cc], w_cs[:, TT - 1:TT])
            v("tensor_mul", w_t1, w_pf, w_cs)
            v("tensor_mul", xna[:, sl], w_t1, ca[:, sl],
              inc=("v" if c == DC - 1 else None))
            if em.dbg is not None and it == 0 and c == 0:
                em.wait("gpsimd", em.tok("v"))
                dtk = [em.dma("gpsimd", em.dbg["S"][0:P, 0:TT], w_s),
                       em.dma("gpsimd", em.dbg["Pf"][0:P, 0:TT], w_pf),
                       em.dma("gpsimd", em.dbg["cs"][0:P, 0:TT], w_cs)]
                em.waits("vector", dtk)
                em.waits("scalar", dtk)
        v("tensor_single_scalar", dta[:, :], xna[:, :], FLT_MAX, ALU.is_le)
        v("tensor_single_scalar", ba[:, :], xna[:, :], -FLT_MAX, ALU.is_ge)
        for c in range(DC):
            sl = slice(c * TT, (c + 1) * TT)
            v("tensor_tensor", w_m, dta[:, sl], ba[:, sl], ALU.mult)
            v("select", ya[:, sl], w_m, xna[:, sl], zeros,
              inc=("v" if c == DC - 1 else None))
        if em.dbg is not None and it == 0:
            em.wait("gpsimd", em.tok("v"))
            dtk = []
            for c in range(DC):
                dtk.append(em.dma("gpsimd", em.dbg["y"][c * P:(c + 1) * P, 0:TT],
                                  xna[:, c * TT:(c + 1) * TT]))
            em.waits("vector", dtk)
            em.waits("scalar", dtk)

        em.wait("scalar", em.tok("v"))
        for c in range(DC):
            s("activation", xna[:, c * TT:(c + 1) * TT], ya[:, c * TT:(c + 1) * TT],
              AF.Square, inc=("s" if c == DC - 1 else None))
        em.wait("tensor", em.tok("v"))
        for c in range(DC):
            t("matmul", pb[0][0:1, 0:TT], ones_c, ya[:, c * TT:(c + 1) * TT],
              start=(c == 0), stop=(c == DC - 1))
        em.wait("tensor", em.tok("s"))
        for c in range(DC):
            t("matmul", pb[1][0:1, 0:TT], ones_c, xna[:, c * TT:(c + 1) * TT],
              start=(c == 0), stop=(c == DC - 1), inc=("t" if c == DC - 1 else None))
        em.wait("vector", em.tok("t"))
        v("tensor_scalar_mul", w_t0[0:1, :], pb[0][0:1, 0:TT], 1.0 / D)
        v("tensor_scalar_mul", w_t1[0:1, :], pb[1][0:1, 0:TT], 1.0 / D)
        v("tensor_mul", w_g[0:1, :], w_t0[0:1, :], w_t0[0:1, :])
        v("tensor_sub", w_t1[0:1, :], w_t1[0:1, :], w_g[0:1, :])
        v("tensor_scalar_add", w_t1[0:1, :], w_t1[0:1, :], 1e-5, inc="v")
        em.wait("scalar", em.tok("v"))
        s("activation", w_cs[0:1, :], w_t1[0:1, :], AF.Sqrt, inc="s")
        em.wait("vector", em.tok("s"))
        v("reciprocal", w_t1[0:1, :], w_cs[0:1, :], inc="v")
        em.wait("tensor", em.tok("v"))
        t("matmul", pb[2][:, 0:TT], ones_r, w_t0[0:1, :], start=True, stop=True)
        t("matmul", pb[3][:, 0:TT], ones_r, w_t1[0:1, :], start=True, stop=True,
          inc="t")
        em.wait("vector", em.tok("t"))
        v("tensor_copy", bc_mu, pb[2][:, 0:TT])
        v("tensor_copy", bc_r, pb[3][:, 0:TT])
        st_toks = []
        for c in range(DC):
            sl = slice(c * TT, (c + 1) * TT)
            v("tensor_sub", ya[:, sl], ya[:, sl], bc_mu)
            v("tensor_mul", ya[:, sl], ya[:, sl], bc_r, inc="v")
            em.wait("gpsimd", em.tok("v"))
            em.wait("gpsimd", g("tensor_add", ya[:, sl], ya[:, sl], xa[:, sl],
                                inc="g"))
            st_toks.append(em.dma("gpsimd", xout[c * P:(c + 1) * P, t0:t0 + TT],
                                  ya[:, sl]))

    ar.pop()
    return st_toks





def emit_ln_stats(em, pb, cst, chunks, sq_chunks, rows, bc_mu, bc_r, TT,
                  wait_first=None):
    s = lambda *a, **k: em.op("scalar", *a, **k)
    v = lambda *a, **k: em.op("vector", *a, **k)
    t = lambda *a, **k: em.op("tensor", *a, **k)
    ones_c, ones_r = cst["ones_c"], cst["ones_r"]
    r0, r1, r2 = rows[0], rows[1], rows[2]
    for c in range(DC):
        s("activation", sq_chunks[c], chunks[c], AF.Square,
          inc=("s" if c == DC - 1 else None))
    if wait_first:
        em.waits("tensor", wait_first)
    em.wait("tensor", em.tok("v"))
    for c in range(DC):
        t("matmul", pb[0][0:1, 0:TT], ones_c, chunks[c],
          start=(c == 0), stop=(c == DC - 1))
    em.wait("tensor", em.tok("s"))
    for c in range(DC):
        t("matmul", pb[1][0:1, 0:TT], ones_c, sq_chunks[c],
          start=(c == 0), stop=(c == DC - 1), inc=("t" if c == DC - 1 else None))
    em.wait("vector", em.tok("t"))
    v("tensor_scalar_mul", r0[0:1, :], pb[0][0:1, 0:TT], 1.0 / D)
    v("tensor_scalar_mul", r1[0:1, :], pb[1][0:1, 0:TT], 1.0 / D)
    v("tensor_mul", r2[0:1, :], r0[0:1, :], r0[0:1, :])
    v("tensor_sub", r1[0:1, :], r1[0:1, :], r2[0:1, :])
    v("tensor_scalar_add", r1[0:1, :], r1[0:1, :], 1e-5, inc="v")
    em.wait("scalar", em.tok("v"))
    s("activation", r2[0:1, :], r1[0:1, :], AF.Sqrt, inc="s")
    em.wait("vector", em.tok("s"))
    v("reciprocal", r1[0:1, :], r2[0:1, :], inc="v")
    em.wait("tensor", em.tok("v"))
    t("matmul", pb[2][:, 0:TT], ones_r, r0[0:1, :], start=True, stop=True)
    t("matmul", pb[3][:, 0:TT], ones_r, r1[0:1, :], start=True, stop=True,
      inc="t")
    em.wait("vector", em.tok("t"))
    v("tensor_copy", bc_mu, pb[2][:, 0:TT])
    v("tensor_copy", bc_r, pb[3][:, 0:TT], inc="v")



def emit_stage2(em, ar, pb, cst, xin, xout, W, ud, hmd, hfd):
    TT = 512
    NT = T // TT
    v = lambda *a, **k: em.op("vector", *a, **k)
    g = lambda *a, **k: em.op("gpsimd", *a, **k)
    s = lambda *a, **k: em.op("scalar", *a, **k)
    t = lambda *a, **k: em.op("tensor", *a, **k)

    ar.push()
    xnb = ar.bf16(DC * T)
    xa = ar.f32(TT * DC)
    sq = ar.f32(TT * DC)
    bc_mu = ar.f32(TT)
    bc_r = ar.f32(TT)
    rows = [ar.f32(TT) for _ in range(3)]
    wp = [ar.bf16(2 * 1024) for _ in range(2)]
    wstg = [ar.f32(1024) for _ in range(2)]
    ustrip = ar.bf16(T)
    u1 = ar.f32(TT)
    u2 = ar.f32(TT)
    scl = ar.f32(64)
    tok = em.dma("sync", scl[:, 0:32], W["w1s_c"])
    tok = em.dma("sync", scl[:, 32:64], W["w2s_c"])
    em.wait("scalar", tok); em.wait("vector", tok)

    ldx_last = []
    for it in range(NT):
        t0 = it * TT
        ld = [em.dma("sync", xa[:, c * TT:(c + 1) * TT],
                     xin[c * P:(c + 1) * P, t0:t0 + TT]) for c in range(DC)]
        em.waits("scalar", ld)
        em.waits("tensor", ld)
        chunks = [xa[:, c * TT:(c + 1) * TT] for c in range(DC)]
        sqs = [sq[:, c * TT:(c + 1) * TT] for c in range(DC)]
        emit_ln_stats(em, pb, cst, chunks, sqs, rows, bc_mu, bc_r, TT)
        for c in range(DC):
            sl = slice(c * TT, (c + 1) * TT)
            v("tensor_sub", sq[:, sl], xa[:, sl], bc_mu)
            v("tensor_mul", sq[:, sl], sq[:, sl], bc_r)
            v("tensor_copy", xnb[:, c * T + t0:c * T + t0 + TT], sq[:, sl],
              inc=("v" if c == DC - 1 else None))
        em.wait("sync", em.tok("v"))
    xn_tok = em.tok("v")

    w1_3 = W["w1T"].rearrange("(c p) o -> p c o", p=P)
    w2_3 = W["w2T"].rearrange("(c p) o -> p c o", p=P)
    conv_tok = {}
    mm_tok = {}

    def pfA(o2):
        par = o2 % 2
        em.wait("sync", conv_tok.get(o2 - 2))
        if o2 - 2 in mm_tok:
            em.wait("vector", mm_tok[o2 - 2])
        else:
            em.wait("vector", em.tok("t"))
        for kk, wsrc in ((0, w1_3), (1, w2_3)):
            pl = em.dma("sync", wstg[par].rearrange("p (c j) -> p c j", j=P),
                        wsrc[:, :, o2 * P:(o2 + 1) * P])
            em.wait("vector", pl)
            v("tensor_copy", wp[par][:, kk * 1024:(kk + 1) * 1024],
              wstg[par][:, 0:1024], inc="v")
            if kk == 0:
                em.wait("sync", em.tok("v"))
        conv_tok[o2] = em.tok("v")

    pfA(0)
    pfA(1)
    for oc in range(32):
        wtok = conv_tok[oc]
        wp_ = wp[oc % 2]
        em.waits("tensor", [wtok, xn_tok, em.tok("s")])
        for tb in range(NT):
            t0 = tb * TT
            bank1 = pb[4 + (tb % 2)]
            bank2 = pb[0 + (tb % 2)]
            if tb >= 2:
                em.wait("tensor", ev[tb - 2])
                em.wait("tensor", evs[tb - 2])
            for c in range(DC):
                t("matmul", bank1[:, 0:TT], wp_[:, c * P:(c + 1) * P],
                  xnb[:, c * T + t0:c * T + t0 + TT],
                  start=(c == 0), stop=(c == DC - 1))
            for c in range(DC):
                t("matmul", bank2[:, 0:TT], wp_[:, 1024 + c * P:1024 + (c + 1) * P],
                  xnb[:, c * T + t0:c * T + t0 + TT],
                  start=(c == 0), stop=(c == DC - 1),
                  inc=("t" if c == DC - 1 else None))
            if tb == 0 and oc == 0:
                ev, evs = {}, {}
            em.wait("scalar", em.tok("t"))
            em.wait("scalar", em.tok("v"))
            s("activation", u1, bank1[:, 0:TT], AF.Silu,
              scale=scl[:, oc:oc + 1], inc="s")
            evs[tb] = em.tok("s")
            em.wait("vector", em.tok("t"))
            v("tensor_scalar_mul", u2, bank2[:, 0:TT], scl[:, 32 + oc:33 + oc],
              inc="v")
            em.wait("vector", em.tok("s"))
            v("tensor_mul", ustrip[:, t0:t0 + TT], u1, u2, inc="v")
            ev[tb] = em.tok("v")
        mm_tok[oc] = em.tok("t")
        if oc + 2 < 32:
            pfA(oc + 2)
        em.wait("gpsimd", em.tok("v"))
        sttok = em.dma("gpsimd", ud[oc * P:(oc + 1) * P, :], ustrip)
        em.wait("vector", sttok)
    em.wait("sync", em.tok("dq_gpsimd"))
    ar.pop()

    def hidden_layer(src_d, dst_d, wname, sname):
        ar.push()
        ut = ar.bf16(32 * 2048)
        wpan = [ar.bf16(32 * P) for _ in range(2)]
        wst = [ar.f32(2048) for _ in range(2)]
        ostrip = ar.bf16(2048)
        scl2 = ar.f32(32)
        tok = em.dma("sync", scl2, W[sname])
        em.wait("scalar", tok)
        em.wait("sync", em.tok("dq_gpsimd"))
        wsrc3 = W[wname].rearrange("(kc p) o -> p kc o", p=P)
        for tbg in range(2):
            tg0 = tbg * 2048
            ld = [em.dma("sync", ut[:, kc * 2048:(kc + 1) * 2048],
                         src_d[kc * P:(kc + 1) * P, tg0:tg0 + 2048])
                  for kc in range(32)]
            conv_tok = {}
            mm_tok = {}

            def pf(o2):
                par = o2 % 2
                em.wait("sync", conv_tok.get(o2 - 2))
                if o2 - 2 in mm_tok:
                    em.wait("vector", mm_tok[o2 - 2])
                else:
                    em.wait("vector", em.tok("t"))
                for half in range(2):
                    pl = em.dma(
                        "sync", wst[par].rearrange("p (kc j) -> p kc j", j=P),
                        wsrc3[:, half * 16:(half + 1) * 16, o2 * P:(o2 + 1) * P])
                    em.wait("vector", pl)
                    v("tensor_copy",
                      wpan[par][:, half * 2048:(half + 1) * 2048],
                      wst[par][:, 0:2048], inc="v")
                    if half == 0:
                        em.wait("sync", em.tok("v"))
                conv_tok[o2] = em.tok("v")

            pf(0)
            pf(1)
            ev = {}
            for oc in range(32):
                em.waits("tensor", [conv_tok[oc]] + ld + [em.tok("s")]
                         if oc == 0 else [conv_tok[oc]])
                wp_ = wpan[oc % 2]
                for ts4 in range(4):
                    t0 = ts4 * TT
                    bank = pb[4 + (ts4 % 2)]
                    key = (oc * 4 + ts4)
                    if key >= 2:
                        em.wait("tensor", ev.get(key - 2))
                    for kc in range(32):
                        t("matmul", bank[:, 0:TT], wp_[:, kc * P:(kc + 1) * P],
                          ut[:, kc * 2048 + t0:kc * 2048 + t0 + TT],
                          start=(kc == 0), stop=(kc == 31),
                          inc=("t" if kc == 31 else None))
                    em.wait("scalar", em.tok("t"))
                    s("activation", ostrip[:, t0:t0 + TT], bank[:, 0:TT], AF.Gelu,
                      scale=scl2[:, oc:oc + 1], inc="s")
                    ev[key] = em.tok("s")
                mm_tok[oc] = em.tok("t")
                if oc + 2 < 32:
                    pf(oc + 2)
                em.wait("gpsimd", em.tok("s"))
                sttok = em.dma("gpsimd", dst_d[oc * P:(oc + 1) * P, tg0:tg0 + 2048],
                               ostrip)
                em.wait("scalar", sttok)
            em.wait("sync", em.tok("t"))
        ar.pop()

    hidden_layer(ud, hmd, "h1T", "h1s_c")
    hidden_layer(hmd, hfd, "h2T", "h2s_c")

    ar.push()
    w3 = ar.bf16(32 * 1024)
    wst = ar.f32(2048)
    hft = ar.bf16(32 * TT)
    ya = ar.f32(TT * DC)
    sq = ar.f32(TT * DC)
    xa = ar.f32(TT * DC)
    bc_mu = ar.f32(TT)
    bc_r = ar.f32(TT)
    rows = [ar.f32(TT) for _ in range(3)]
    scl3 = ar.f32(8)
    em.wait("sync", em.tok("dq_gpsimd"))
    tok = em.dma("sync", scl3, W["w3s_c"])
    em.wait("vector", tok)
    for kc in range(32):
        em.wait("sync", em.tok("v"))
        tok = em.dma("sync", wst[:, 0:1024], W["w3T"][kc * P:(kc + 1) * P, :])
        em.wait("vector", tok)
        v("tensor_copy", w3[:, kc * 1024:(kc + 1) * 1024], wst[:, 0:1024], inc="v")
    w3tok = em.tok("v")
    em.wait("tensor", w3tok)
    st_toks = []
    for it in range(NT):
        t0 = it * TT
        em.wait("tensor", em.tok("v"))
        em.wait("tensor", em.tok("g"))
        em.waits("vector", st_toks)
        em.waits("scalar", st_toks)
        if it > 0:
            em.wait("sync", em.tok("g"))
        ldh = [em.dma("sync", hft[:, kc * TT:(kc + 1) * TT],
                      hfd[kc * P:(kc + 1) * P, t0:t0 + TT]) for kc in range(32)]
        ldx = [em.dma("sync", xa[:, c * TT:(c + 1) * TT],
                      xin[c * P:(c + 1) * P, t0:t0 + TT]) for c in range(DC)]
        em.waits("tensor", ldh)
        ev2 = {}
        for oc in range(DC):
            bank = pb[4 + (oc % 2)]
            if oc >= 2:
                em.wait("tensor", ev2[oc - 2])
            for kc in range(32):
                t("matmul", bank[:, 0:TT], w3[:, kc * 1024 + oc * P:kc * 1024 + (oc + 1) * P],
                  hft[:, kc * TT:(kc + 1) * TT], start=(kc == 0), stop=(kc == 31),
                  inc=("t" if kc == 31 else None))
            em.wait("vector", em.tok("t"))
            v("tensor_scalar_mul", ya[:, oc * TT:(oc + 1) * TT], bank[:, 0:TT],
              scl3[:, oc:oc + 1], inc="v")
            ev2[oc] = em.tok("v")
        if em.dbg is not None and it == 0:
            em.wait("gpsimd", em.tok("v"))
            ytk = []
            for c in range(DC):
                ytk.append(em.dma("gpsimd",
                                  em.dbg["y"][c * P:(c + 1) * P, 0:TT],
                                  ya[:, c * TT:(c + 1) * TT]))
            em.waits("vector", ytk)
        chunks = [ya[:, c * TT:(c + 1) * TT] for c in range(DC)]
        sqs = [sq[:, c * TT:(c + 1) * TT] for c in range(DC)]
        em.wait("scalar", em.tok("v"))
        emit_ln_stats(em, pb, cst, chunks, sqs, rows, bc_mu, bc_r, TT)
        em.waits("gpsimd", ldx)
        st_toks = []
        for c in range(DC):
            sl = slice(c * TT, (c + 1) * TT)
            v("tensor_sub", ya[:, sl], ya[:, sl], bc_mu)
            v("tensor_mul", ya[:, sl], ya[:, sl], bc_r, inc="v")
            em.wait("gpsimd", em.tok("v"))
            em.wait("gpsimd", g("tensor_add", ya[:, sl], ya[:, sl], xa[:, sl],
                                inc="g"))
            st_toks.append(em.dma("gpsimd", xout[c * P:(c + 1) * P, t0:t0 + TT],
                                  ya[:, sl]))
    ar.pop()
    return st_toks


def host_inputs_stage2(gg):
    h = {}
    h["w1T"] = np.ascontiguousarray(gg["L0_ffn_w1_w"].T)
    h["w2T"] = np.ascontiguousarray(gg["L0_ffn_w2_w"].T)
    h["h1T"] = np.ascontiguousarray(gg["L0_ffn_h1_w"].T)
    h["h2T"] = np.ascontiguousarray(gg["L0_ffn_h2_w"].T)
    h["w3T"] = np.ascontiguousarray(gg["L0_ffn_w3_w"].T)
    h["w1s_c"] = np.ascontiguousarray(gg["L0_ffn_w1_s"].reshape(32, P).T)
    h["w2s_c"] = np.ascontiguousarray(gg["L0_ffn_w2_s"].reshape(32, P).T)
    h["h1s_c"] = np.ascontiguousarray(gg["L0_ffn_h1_s"].reshape(32, P).T)
    h["h2s_c"] = np.ascontiguousarray(gg["L0_ffn_h2_s"].reshape(32, P).T)
    h["w3s_c"] = np.ascontiguousarray(gg["L0_ffn_w3_s"].reshape(DC, P).T)
    return h





def emit_stage3(em, ar, pb, pbT, cst, xin, xout, W):
    U = 256
    NST = T // U
    CH = 128
    NCH = U // CH
    v = lambda *a, **k: em.op("vector", *a, **k)
    g = lambda *a, **k: em.op("gpsimd", *a, **k)
    s = lambda *a, **k: em.op("scalar", *a, **k)
    t = lambda *a, **k: em.op("tensor", *a, **k)

    def vw(*a, **k):
        em.wait("vector", v(*a, inc="v", **k))

    ar.push()
    fb = ar.f32(8)

    def vfence():
        v("memset", fb[:, 0:1], 0.0, inc="v")
        em.wait("vector", em.tok("v"))

    def sfence():
        s("memzero", fb[:, 1:2], inc="s")
        em.wait("scalar", em.tok("s"))
    wq_sb = ar.bf16(DC * 1024)
    wk_sb = ar.bf16(DC * 1024)
    wv_sb = ar.bf16(DC * 1024)
    wo_sb = ar.bf16(DC * 1024)
    wbeta_sb = ar.bf16(DC * 16)
    x2t = ar.f32(U * DC)
    xs = ar.f32(U * DC)
    xo = ar.f32(U * DC)
    sq = ar.f32(U * DC)
    yo = ar.f32(U * DC)
    xob = ar.bf16(U * DC)
    knb = ar.bf16(U * DC)
    vnb = ar.bf16(U * DC)
    qb = ar.bf16(U * DC)
    kb = ar.bf16(U * DC)
    vb = ar.bf16(U * DC)
    ob = ar.bf16(U * DC)
    vt_all = ar.bf16(U * DC)
    kst = ar.bf16(DC * CH)
    qs = ar.bf16(DC * CH)
    ks = ar.bf16(DC * CH)
    asb = ar.bf16(2 * CH)
    bcq = ar.f32(DC * CH)
    bck = ar.f32(DC * CH)
    plbc = ar.f32(512)
    mst = ar.f32(512)
    mstb = ar.bf16(512)
    mtmp = ar.f32(512)
    beta = ar.f32(U)
    brow = ar.f32(8 * CH)
    bc_mu = ar.f32(U)
    bc_r = ar.f32(U)
    rows = [ar.f32(U) for _ in range(3)]
    zeros = ar.f32(512)
    cols = ar.f32(64)
    esel = ar.f32(1024)
    epar = ar.f32(128)
    d64 = ar.f32(512)
    triu = ar.f32(128)
    wstg = ar.f32(1024)

    toks = [em.dma("sync", esel[0:16, :], W["c_esel"]),
            em.dma("sync", epar[0:16, :], W["c_epar"]),
            em.dma("sync", d64[0:16, :], W["c_d64"]),
            em.dma("sync", triu, W["c_triu"]),
            em.dma("sync", cols[:, 0:8], W["rin_c"]),
            em.dma("sync", cols[:, 8:16], W["rk_c"]),
            em.dma("sync", cols[:, 16:24], W["rv_c"]),
            em.dma("sync", cols[:, 24:32], W["bo_c"]),
            em.dma("sync", cols[:, 32:40], W["lnw_c"]),
            em.dma("sync", cols[:, 40:48], W["lnb_c"]),
            em.dma("sync", cols[0:16, 48:49], W["betab_c"]),
            em.dma("sync", cols[0:16, 49:50], W["basebeta_c"])]
    em.waits("vector", toks)
    em.waits("scalar", toks)
    vw("memset", zeros, 0.0)
    vw("memset", mst, 0.0)
    vw("memset", mstb, 0.0)
    vw("memset", cols[:, 50:51], 0.0)
    em.wait("scalar", em.tok("v"))
    s("activation", cols[0:16, 51:52], cols[0:16, 49:50], AF.Sigmoid, inc="s")
    em.wait("vector", em.tok("s"))

    for wname, dst in (("WqT", wq_sb), ("WkT", wk_sb), ("WvT", wv_sb),
                       ("WoT", wo_sb)):
        for c in range(DC):
            em.wait("sync", em.tok("v"))
            tk = em.dma("sync", wstg, W[wname][c * P:(c + 1) * P, :])
            em.wait("vector", tk)
            v("tensor_copy", dst[:, c * 1024:(c + 1) * 1024], wstg, inc="v")
    for c in range(DC):
        em.wait("sync", em.tok("v"))
        tk = em.dma("sync", wstg[:, 0:16], W["betawT"][c * P:(c + 1) * P, :])
        em.wait("vector", tk)
        v("tensor_copy", wbeta_sb[:, c * 16:(c + 1) * 16], wstg[:, 0:16], inc="v")

    rin_c, rk_c, rv_c = cols[:, 0:8], cols[:, 8:16], cols[:, 16:24]
    bo_c, lnw_c, lnb_c = cols[:, 24:32], cols[:, 32:40], cols[:, 40:48]
    betab, sigbase, din = cols[0:16, 48:49], cols[0:16, 51:52], cols[0:16, 50:51]

    def rms_bc(src_chunks, eps):
        em.wait("scalar", em.tok("v"))
        for c in range(DC):
            s("activation", sq[:, c * U:(c + 1) * U], src_chunks[c], AF.Square,
              inc=("s" if c == DC - 1 else None))
        em.wait("tensor", em.tok("s"))
        em.wait("tensor", em.tok("v"))
        for c in range(DC):
            t("matmul", pb[0][0:1, 0:U], cst["ones_c"], sq[:, c * U:(c + 1) * U],
              start=(c == 0), stop=(c == DC - 1), inc=("t" if c == DC - 1 else None))
        em.wait("vector", em.tok("t"))
        v("tensor_scalar", rows[0][0:1, :], pb[0][0:1, 0:U], 1.0 / D, eps,
          ALU.mult, ALU.add, inc="v")
        em.wait("scalar", em.tok("v"))
        s("activation", rows[1][0:1, :], rows[0][0:1, :], AF.Sqrt, inc="s")
        em.wait("vector", em.tok("s"))
        v("reciprocal", rows[0][0:1, :], rows[1][0:1, :], inc="v")
        em.wait("tensor", em.tok("v"))
        t("matmul", pb[2][:, 0:U], cst["ones_r"], rows[0][0:1, :], start=True,
          stop=True, inc="t")
        em.wait("vector", em.tok("t"))
        v("tensor_copy", bc_r, pb[2][:, 0:U])

    st_toks = []
    for st in range(NST):
        t0 = st * U
        em.wait("tensor", em.tok("v"))
        em.wait("tensor", em.tok("g"))
        em.wait("tensor", em.tok("s"))
        em.waits("vector", st_toks)
        em.waits("scalar", st_toks)
        em.wait("gpsimd", em.tok("v"))
        if st > 0:
            em.wait("sync", em.tok("g"))
        ld = [em.dma("sync", x2t[:, c * U:(c + 1) * U],
                     xin[c * P:(c + 1) * P, t0:t0 + U]) for c in range(DC)]
        em.waits("scalar", ld)
        em.waits("tensor", ld)
        em.waits("vector", ld)
        chunks0 = [x2t[:, c * U:(c + 1) * U] for c in range(DC)]
        sqs0 = [sq[:, c * U:(c + 1) * U] for c in range(DC)]
        em.wait("scalar", em.tok("v"))
        emit_ln_stats(em, pb, cst, chunks0, sqs0, rows, bc_mu, bc_r, U)
        for c in range(DC):
            sl = slice(c * U, (c + 1) * U)
            v("tensor_sub", xs[:, sl], x2t[:, sl], bc_mu)
            v("tensor_mul", xs[:, sl], xs[:, sl], bc_r,
              inc=("v" if c == DC - 1 else None))
        rms_bc([xs[:, c * U:(c + 1) * U] for c in range(DC)], 1e-6)
        for c in range(DC):
            sl = slice(c * U, (c + 1) * U)
            v("tensor_mul", xo[:, sl], xs[:, sl], bc_r)
            v("tensor_scalar_mul", xo[:, sl], xo[:, sl], rin_c[:, c:c + 1])
            v("tensor_copy", xob[:, sl], xo[:, sl],
              inc=("v" if c == DC - 1 else None))
        rms_bc([xo[:, c * U:(c + 1) * U] for c in range(DC)], 1e-6)
        for c in range(DC):
            sl = slice(c * U, (c + 1) * U)
            v("tensor_mul", sq[:, sl], xo[:, sl], bc_r)
            v("tensor_scalar_mul", knb[:, sl], sq[:, sl], rk_c[:, c:c + 1])
            v("tensor_scalar_mul", vnb[:, sl], sq[:, sl], rv_c[:, c:c + 1],
              inc=("v" if c == DC - 1 else None))
        intok = em.tok("v")

        em.wait("tensor", intok)
        ev = {}
        j = 0
        for wsb, src_b, dst_b in ((wq_sb, xob, qb), (wk_sb, knb, kb),
                                  (wv_sb, vnb, vb)):
            for oc in range(DC):
                bank = pb[4 + (j % 2)]
                if j >= 2:
                    em.wait("tensor", ev[j - 2])
                for c in range(DC):
                    t("matmul", bank[:, 0:U],
                      wsb[:, c * 1024 + oc * P:c * 1024 + (oc + 1) * P],
                      src_b[:, c * U:(c + 1) * U], start=(c == 0),
                      stop=(c == DC - 1), inc=("t" if c == DC - 1 else None))
                em.wait("vector", em.tok("t"))
                v("tensor_copy", dst_b[:, oc * U:(oc + 1) * U], bank[:, 0:U],
                  inc="v")
                ev[j] = em.tok("v")
                j += 1
        bank = pb[4 + (j % 2)]
        em.wait("tensor", ev[j - 2])
        for c in range(DC):
            t("matmul", bank[0:16, 0:U], wbeta_sb[:, c * 16:(c + 1) * 16],
              xob[:, c * U:(c + 1) * U], start=(c == 0), stop=(c == DC - 1),
              inc=("t" if c == DC - 1 else None))
        em.wait("scalar", em.tok("t"))
        em.wait("scalar", em.tok("v"))
        s("activation", beta[0:16, :], bank[0:16, 0:U], AF.Sigmoid,
          bias=betab, inc="s")
        sfence()
        em.wait("vector", em.tok("s"))
        vw("tensor_scalar", beta[0:16, :], beta[0:16, :], sigbase, 0.9995,
           ALU.mult, ALU.min)
        vw("tensor_scalar_max", beta[0:16, :], beta[0:16, :], 0.9)

        em.wait("tensor", em.tok("v"))
        for ch in range(NCH):
            for c in range(DC):
                slot = (c % 4) * 256
                t("transpose", pbT[:, slot:slot + 128],
                  vb[:, c * U + ch * CH:c * U + ch * CH + CH], cst["identb"],
                  inc="t")
                em.wait("vector", em.tok("t"))
                v("tensor_copy",
                  vt_all[:, ch * 1024 + c * CH:ch * 1024 + (c + 1) * CH],
                  pbT[:, slot:slot + 128], inc="v")
                em.wait("tensor", em.tok("v"))
        vfence()

        for ch in range(NCH):
            cb = ch * CH
            bsl = beta[0:16, cb:cb + CH]
            r0 = brow[0:16, 0:CH]
            r1 = brow[0:16, CH:2 * CH]
            r2 = brow[0:16, 2 * CH:3 * CH]
            r3 = brow[0:16, 3 * CH:4 * CH]
            r4 = brow[0:16, 4 * CH:5 * CH]
            r5 = brow[0:16, 5 * CH:6 * CH]
            plc = brow[0:16, 6 * CH:6 * CH + 1]
            vw("tensor_tensor_scan", r0, bsl, zeros[0:16, 0:CH], 1.0, ALU.mult,
               ALU.add)
            vw("reciprocal", r1, r0)
            vw("tensor_tensor_scan", r2, r1, zeros[0:16, 0:CH], 0.0, ALU.add,
               ALU.add)
            vw("tensor_scalar_add", r3, r2, din)
            vw("tensor_mul", r3, r3, r0)
            vw("tensor_copy", din, r3[0:16, CH - 1:CH])
            vw("tensor_scalar_add", r5, r3, 1e-6)
            vw("reciprocal", r5, r5)
            vw("tensor_mul", r4, r5, r0)
            vw("tensor_copy", plc, r0[0:16, CH - 1:CH])
            em.wait("tensor", em.tok("v"))
            for c in range(DC):
                bank = pb[0 + c // 4]
                t("matmul", bank[:, (c % 4) * CH:(c % 4 + 1) * CH],
                  esel[0:16, c * P:(c + 1) * P], r4, start=True, stop=True,
                  inc=("t" if c == DC - 1 else None))
            em.wait("vector", em.tok("t"))
            v("tensor_copy", bcq[:, 0:512], pb[0][:, 0:512])
            v("tensor_copy", bcq[:, 512:1024], pb[1][:, 0:512], inc="v")
            em.wait("tensor", em.tok("v"))
            for c in range(DC):
                bank = pb[0 + c // 4]
                t("matmul", bank[:, (c % 4) * CH:(c % 4 + 1) * CH],
                  esel[0:16, c * P:(c + 1) * P], r1, start=True, stop=True,
                  inc=("t" if c == DC - 1 else None))
            em.wait("vector", em.tok("t"))
            v("tensor_copy", bck[:, 0:512], pb[0][:, 0:512])
            v("tensor_copy", bck[:, 512:1024], pb[1][:, 0:512], inc="v")
            vw("tensor_scalar_mul", mtmp[0:16, 0:512], d64[0:16, :], plc)
            vfence()
            em.wait("tensor", em.tok("v"))
            t("matmul", pb[0][:, 0:512], epar[0:16, :], mtmp[0:16, 0:512],
              start=True, stop=True, inc="t")
            em.wait("vector", em.tok("t"))
            v("tensor_copy", plbc, pb[0][:, 0:512], inc="v")
            for c in range(DC):
                sl = slice(c * U + cb, c * U + cb + CH)
                v("tensor_mul", qs[:, c * CH:(c + 1) * CH], qb[:, sl],
                  bcq[:, c * CH:(c + 1) * CH])
                v("tensor_mul", ks[:, c * CH:(c + 1) * CH], kb[:, sl],
                  bck[:, c * CH:(c + 1) * CH],
                  inc=("v" if c == DC - 1 else None))
            vfence()
            em.wait("tensor", em.tok("v"))
            for c in range(DC):
                slot = (c % 4) * 256
                t("transpose", pbT[:, slot:slot + 128],
                  ks[:, c * CH:(c + 1) * CH], cst["identb"], inc="t")
                em.wait("vector", em.tok("t"))
                v("tensor_copy", kst[:, c * CH:(c + 1) * CH],
                  pbT[:, slot:slot + 128], inc="v")
                em.wait("tensor", em.tok("v"))
            vfence()
            for pc in range(DC):
                csl = slice(pc * CH, (pc + 1) * CH)
                t("matmul", pb[2][:, 0:CH], ks[0:64, csl], qs[0:64, csl],
                  start=True, stop=True)
                t("matmul", pb[3][:, 0:CH], ks[64:128, csl], qs[64:128, csl],
                  start=True, stop=True, inc="t")
                em.wait("vector", em.tok("t"))
                v("tensor_mul", asb[:, 0:CH], pb[2][:, 0:CH], triu)
                v("tensor_mul", asb[:, CH:2 * CH], pb[3][:, 0:CH], triu, inc="v")
                vfence()
                em.wait("tensor", em.tok("v"))
                obank = pb[4 + pc // 4]
                osl = slice((pc % 4) * CH, (pc % 4) * CH + CH)
                vtb = ch * 1024 + pc * CH
                for hh in range(2):
                    b0 = 64 * hh
                    t("matmul", obank[b0:b0 + 64, osl],
                      vt_all[:, vtb + b0:vtb + b0 + 64],
                      asb[:, hh * CH:(hh + 1) * CH], start=True, stop=False,
                      tile_position=(0, b0))
                    t("matmul", obank[b0:b0 + 64, osl],
                      mstb[b0:b0 + 64, 64 * pc:64 * pc + 64],
                      qs[b0:b0 + 64, csl], start=False, stop=True,
                      tile_position=(b0, b0))
                    t("matmul", pb[6][b0:b0 + 64, 64 * pc:64 * pc + 64],
                      kst[:, pc * CH + b0:pc * CH + b0 + 64],
                      vt_all[:, vtb + b0:vtb + b0 + 64],
                      start=True, stop=True, tile_position=(0, b0),
                      inc=("t" if (pc == DC - 1 and hh == 1) else None))
            em.wait("vector", em.tok("t"))
            for c in range(DC):
                obank = pb[4 + c // 4]
                v("tensor_copy", ob[:, c * U + cb:c * U + cb + CH],
                  obank[:, (c % 4) * CH:(c % 4 + 1) * CH],
                  inc=("v" if c == DC - 1 else None))
            v("tensor_add", mtmp[:, 0:512], mst[:, 0:512], pb[6][:, 0:512])
            v("tensor_mul", mst[:, 0:512], mtmp[:, 0:512], plbc)
            v("tensor_copy", mstb[:, 0:512], mst[:, 0:512], inc="v")
            vfence()
            em.wait("tensor", em.tok("v"))

        ev2 = {}
        for oc in range(DC):
            bank = pb[4 + (oc % 2)]
            if oc >= 2:
                em.wait("tensor", ev2[oc - 2])
            for c in range(DC):
                t("matmul", bank[:, 0:U],
                  wo_sb[:, c * 1024 + oc * P:c * 1024 + (oc + 1) * P],
                  ob[:, c * U:(c + 1) * U], start=(c == 0), stop=(c == DC - 1),
                  inc=("t" if c == DC - 1 else None))
            em.wait("vector", em.tok("t"))
            v("tensor_scalar_add", yo[:, oc * U:(oc + 1) * U], bank[:, 0:U],
              bo_c[:, oc:oc + 1])
            v("tensor_add", yo[:, oc * U:(oc + 1) * U],
              yo[:, oc * U:(oc + 1) * U], xs[:, oc * U:(oc + 1) * U], inc="v")
            ev2[oc] = em.tok("v")
        chunks = [yo[:, c * U:(c + 1) * U] for c in range(DC)]
        sqs = [sq[:, c * U:(c + 1) * U] for c in range(DC)]
        em.wait("scalar", em.tok("v"))
        emit_ln_stats(em, pb, cst, chunks, sqs, rows, bc_mu, bc_r, U)
        for c in range(DC):
            sl = slice(c * U, (c + 1) * U)
            v("tensor_sub", yo[:, sl], yo[:, sl], bc_mu)
            v("tensor_mul", yo[:, sl], yo[:, sl], bc_r)
            v("tensor_scalar", yo[:, sl], yo[:, sl], lnw_c[:, c:c + 1],
              lnb_c[:, c:c + 1], ALU.mult, ALU.add,
              inc=("v" if c == DC - 1 else None))
        em.wait("scalar", em.tok("v"))
        emit_ln_stats(em, pb, cst, chunks, sqs, rows, bc_mu, bc_r, U)
        st_toks = []
        for c in range(DC):
            sl = slice(c * U, (c + 1) * U)
            v("tensor_sub", yo[:, sl], yo[:, sl], bc_mu)
            v("tensor_mul", yo[:, sl], yo[:, sl], bc_r, inc="v")
            em.wait("gpsimd", em.tok("v"))
            em.wait("gpsimd", g("tensor_add", yo[:, sl], yo[:, sl],
                                x2t[:, sl], inc="g"))
            st_toks.append(em.dma("gpsimd", xout[c * P:(c + 1) * P, t0:t0 + U],
                                  yo[:, sl]))
    ar.pop()
    return st_toks


def host_inputs_stage3(gg):
    h = {}
    h["WqT"] = np.ascontiguousarray(gg["L1_attn_Wq"].T)
    h["WkT"] = np.ascontiguousarray(gg["L1_attn_Wk"].T)
    h["WvT"] = np.ascontiguousarray(gg["L1_attn_Wv"].T)
    h["WoT"] = np.ascontiguousarray(gg["L1_attn_Wo"].T)
    h["betawT"] = np.ascontiguousarray(gg["L1_attn_beta_w"].T)
    h["rin_c"] = np.ascontiguousarray(gg["L1_attn_rin_w"].reshape(DC, P).T)
    h["rk_c"] = np.ascontiguousarray(gg["L1_attn_rk_w"].reshape(DC, P).T)
    h["rv_c"] = np.ascontiguousarray(gg["L1_attn_rv_w"].reshape(DC, P).T)
    h["bo_c"] = np.ascontiguousarray(gg["L1_attn_bo"].reshape(DC, P).T)
    h["lnw_c"] = np.ascontiguousarray(gg["L1_attn_ln_w"].reshape(DC, P).T)
    h["lnb_c"] = np.ascontiguousarray(gg["L1_attn_ln_b"].reshape(DC, P).T)
    h["betab_c"] = np.ascontiguousarray(gg["L1_attn_beta_b"].reshape(16, 1))
    h["basebeta_c"] = np.ascontiguousarray(
        gg["L1_attn_base_beta_logit"].reshape(16, 1))
    NH = 16
    esel = np.zeros((16, 1024), np.float32)
    for p in range(1024):
        esel[p // 64, p] = 1.0
    h["c_esel"] = esel
    epar = np.zeros((16, 128), np.float32)
    for hh in range(16):
        for p in range(128):
            if hh % 2 == p // 64:
                epar[hh, p] = 1.0
    h["c_epar"] = epar
    d64 = np.zeros((16, 512), np.float32)
    for hh in range(16):
        b = hh // 2
        d64[hh, 64 * b:64 * (b + 1)] = 1.0
    h["c_d64"] = d64
    h["c_triu"] = np.triu(np.ones((128, 128), np.float32))
    h["identb_d"] = np.eye(128, dtype=ml_dtypes.bfloat16)
    return h



def emit_stage4(em, ar, pb, pbT, cst, xin, xout, W, out_tok=None):
    TT = 256
    NT = T // TT
    NTC = TT // P
    v = lambda *a, **k: em.op("vector", *a, **k)
    def vw(*a, **k):
        em.wait("vector", v(*a, inc="v", **k))
    g = lambda *a, **k: em.op("gpsimd", *a, **k)
    s = lambda *a, **k: em.op("scalar", *a, **k)
    t = lambda *a, **k: em.op("tensor", *a, **k)

    ar.push()
    fb = ar.f32(8)

    def vfence():
        v("memset", fb[:, 0:1], 0.0, inc="v")
        em.wait("vector", em.tok("v"))

    def sfence():
        s("memzero", fb[:, 1:2], inc="s")
        em.wait("scalar", em.tok("s"))

    wi_sb = ar.bf16(DC * 4096)
    wo_sb = ar.bf16(32 * 1024)
    gate_sb = ar.bf16(DC * 8)
    lA_sb = ar.bf16(DC * 32)
    lB_sb = ar.bf16(4096)
    inter = ar.bf16(32 * TT)
    xa = ar.f32(TT * DC)
    sq = ar.f32(TT * DC)
    ya = ar.f32(TT * DC)
    xnb = ar.bf16(TT * DC)
    glog = ar.f32(NTC * 8)
    gmax = ar.f32(NTC)
    gm = ar.f32(NTC * 8)
    gcum = ar.f32(NTC * 8)
    sl_tm = ar.f32(NTC * 32)
    zb = ar.bf16(NTC * 128)
    zt = ar.bf16(TT)
    bc_mu = ar.f32(TT)
    bc_r = ar.f32(TT)
    rows = [ar.f32(TT) for _ in range(3)]
    cols = ar.f32(64)
    gbbc = ar.f32(8)
    wstg = ar.f32(2048)
    obuf = inter.bitcast(F32)[:, 0:2048]
    obk_tok = {}

    toks = [em.dma("sync", cols[:, 0:32], W["wib_c"]),
            em.dma("sync", cols[:, 32:40], W["wob_c"]),
            em.dma("sync", gbbc, W["gateb_bc"])]
    em.waits("vector", toks)
    em.waits("scalar", toks)
    for c in range(DC):
        for hh in range(2):
            em.wait("sync", em.tok("v"))
            tk = em.dma("sync", wstg,
                        W["wiT"][c * P:(c + 1) * P, hh * 2048:(hh + 1) * 2048])
            em.wait("vector", tk)
            v("tensor_copy",
              wi_sb[:, c * 4096 + hh * 2048:c * 4096 + (hh + 1) * 2048], wstg,
              inc="v")
    for kc in range(32):
        em.wait("sync", em.tok("v"))
        tk = em.dma("sync", wstg[:, 0:1024], W["woT"][kc * P:(kc + 1) * P, :])
        em.wait("vector", tk)
        v("tensor_copy", wo_sb[:, kc * 1024:(kc + 1) * 1024], wstg[:, 0:1024],
          inc="v")
    for c in range(DC):
        em.wait("sync", em.tok("v"))
        tk1 = em.dma("sync", wstg[:, 0:8], W["gateT"][c * P:(c + 1) * P, :])
        tk2 = em.dma("sync", wstg[:, 8:40], W["loraAT"][c * P:(c + 1) * P, :])
        em.waits("vector", [tk1, tk2])
        v("tensor_copy", gate_sb[:, c * 8:(c + 1) * 8], wstg[:, 0:8])
        v("tensor_copy", lA_sb[:, c * 32:(c + 1) * 32], wstg[:, 8:40], inc="v")
    for hh in range(2):
        em.wait("sync", em.tok("v"))
        tk = em.dma("sync", wstg[0:32, 0:2048],
                    W["loraBT"][:, hh * 2048:(hh + 1) * 2048])
        em.wait("vector", tk)
        v("tensor_copy", lB_sb[0:32, hh * 2048:(hh + 1) * 2048],
          wstg[0:32, 0:2048], inc="v")
    wtok = em.tok("v")

    st_toks = []
    for it in range(NT):
        t0 = it * TT
        em.wait("tensor", em.tok("v"))
        em.wait("tensor", em.tok("g"))
        em.wait("tensor", em.tok("s"))
        em.waits("vector", st_toks)
        em.waits("scalar", st_toks)
        em.wait("gpsimd", em.tok("v"))
        if it > 0:
            em.wait("sync", em.tok("g"))
            em.wait("sync", em.tok("v"))
        ld = [em.dma("sync", xa[:, c * TT:(c + 1) * TT],
                     xin[c * P:(c + 1) * P, t0:t0 + TT]) for c in range(DC)]
        em.waits("scalar", ld)
        em.waits("tensor", ld)
        em.waits("vector", ld)
        chunks = [xa[:, c * TT:(c + 1) * TT] for c in range(DC)]
        sqs = [sq[:, c * TT:(c + 1) * TT] for c in range(DC)]
        em.wait("scalar", em.tok("v"))
        emit_ln_stats(em, pb, cst, chunks, sqs, rows, bc_mu, bc_r, TT)
        for c in range(DC):
            sl = slice(c * TT, (c + 1) * TT)
            v("tensor_sub", sq[:, sl], xa[:, sl], bc_mu)
            v("tensor_mul", sq[:, sl], sq[:, sl], bc_r)
            v("tensor_copy", xnb[:, sl], sq[:, sl],
              inc=("v" if c == DC - 1 else None))
        vfence()
        xntok = em.tok("v")
        em.wait("tensor", xntok)
        for tc in range(NTC):
            for c in range(DC):
                t("matmul", pb[0][:, 0:8],
                  xnb[:, c * TT + tc * P:c * TT + (tc + 1) * P],
                  gate_sb[:, c * 8:(c + 1) * 8], start=(c == 0),
                  stop=(c == DC - 1))
            for c in range(DC):
                t("matmul", pb[1][:, 0:32],
                  xnb[:, c * TT + tc * P:c * TT + (tc + 1) * P],
                  lA_sb[:, c * 32:(c + 1) * 32], start=(c == 0),
                  stop=(c == DC - 1), inc=("t" if c == DC - 1 else None))
            em.wait("vector", em.tok("t"))
            gsl = slice(tc * 8, (tc + 1) * 8)
            vw("tensor_tensor", glog[:, gsl], pb[0][:, 0:8], gbbc, ALU.add)
            vw("tensor_reduce", gmax[:, tc:tc + 1], glog[:, gsl], AX, ALU.max)
            vw("tensor_scalar", gm[:, gsl], glog[:, gsl], gmax[:, tc:tc + 1],
               None, ALU.is_ge)
            vw("tensor_tensor_scan", gcum[:, gsl], gm[:, gsl],
               gm[:, gsl], 0.0, ALU.add, ALU.bypass)
            vw("tensor_single_scalar", gcum[:, gsl], gcum[:, gsl], 1.0,
               ALU.is_equal)
            vw("tensor_mul", gm[:, gsl], gm[:, gsl], gcum[:, gsl])
            vw("tensor_copy", sl_tm[:, tc * 32:(tc + 1) * 32], pb[1][:, 0:32])
            v("memset", zb[:, tc * 128:(tc + 1) * 128], 0.0)
            zblk = zb[:, tc * 128:tc * 128 + 32]
            z3 = zblk.rearrange("p (e r) -> p e r", r=4)
            for r in range(3):
                v("tensor_copy", z3[:, :, r:r + 1], gm[:, gsl])
            vw("tensor_copy", z3[:, :, 3:4], gm[:, gsl])
            vw("tensor_tensor", zblk, zblk, sl_tm[:, tc * 32:(tc + 1) * 32],
               ALU.mult)
            vfence()
            em.wait("tensor", em.tok("v"))
            t("transpose", pbT[:, 0:128], zb[:, tc * 128:(tc + 1) * 128],
              cst["identb"], inc="t")
            em.wait("vector", em.tok("t"))
            v("tensor_copy", zt[0:32, tc * P:(tc + 1) * P], pbT[0:32, 0:128],
              inc="v")
            vfence()
        em.wait("tensor", em.tok("v"))
        ev = {}
        for fc in range(32):
            bank = pb[4 + (fc % 2)]
            if fc >= 2:
                em.wait("tensor", ev[fc - 2])
            for c in range(DC):
                t("matmul", bank[:, 0:TT],
                  wi_sb[:, c * 4096 + fc * P:c * 4096 + (fc + 1) * P],
                  xnb[:, c * TT:(c + 1) * TT], start=(c == 0), stop=False)
            t("matmul", bank[:, 0:TT], lB_sb[0:32, fc * P:(fc + 1) * P],
              zt[0:32, 0:TT], start=False, stop=True,
              inc="t")
            em.wait("scalar", em.tok("t"))
            em.wait("scalar", em.tok("v"))
            s("activation", inter[:, fc * TT:(fc + 1) * TT], bank[:, 0:TT],
              AF.Relu, bias=cols[:, fc:fc + 1], inc="s")
            ev[fc] = em.tok("s")
        sfence()
        em.wait("tensor", em.tok("s"))
        ev2 = {}
        for oc in range(DC):
            bank = pb[4 + (oc % 2)]
            if oc >= 2:
                em.wait("tensor", ev2[oc - 2])
            for kc in range(32):
                t("matmul", bank[:, 0:TT],
                  wo_sb[:, kc * 1024 + oc * P:kc * 1024 + (oc + 1) * P],
                  inter[:, kc * TT:(kc + 1) * TT], start=(kc == 0),
                  stop=(kc == 31), inc=("t" if kc == 31 else None))
            em.wait("vector", em.tok("t"))
            v("tensor_scalar_add", ya[:, oc * TT:(oc + 1) * TT], bank[:, 0:TT],
              cols[:, 32 + oc:33 + oc], inc="v")
            ev2[oc] = em.tok("v")
        if em.dbg is not None and it == 0:
            em.wait("gpsimd", em.tok("v"))
            ytk = []
            for c in range(DC):
                ytk.append(em.dma("gpsimd",
                                  em.dbg["y"][c * P:(c + 1) * P, 0:TT],
                                  ya[:, c * TT:(c + 1) * TT]))
            em.waits("vector", ytk)
            em.dma("gpsimd", em.dbg["S"][0:P, 0:16],
                   glog[:, 0:16])
            em.dma("gpsimd", em.dbg["S"][0:P, 16:32], gm[:, 0:16])
            em.dma("gpsimd", em.dbg["k"][0:P, 0:256], zb[:, 0:256])
            em.dma("gpsimd", em.dbg["k"][128:160, 256:512], zt[0:32, 0:256])
            for kc in range(8):
                em.dma("gpsimd", em.dbg["Pf"][kc * P:(kc + 1) * P, 0:TT],
                       inter.bitcast(F32)[:, kc * TT // 2:(kc + 1) * TT // 2]
                       if False else inter[:, kc * TT:(kc + 1) * TT])
        chunks = [ya[:, c * TT:(c + 1) * TT] for c in range(DC)]
        em.wait("scalar", em.tok("v"))
        emit_ln_stats(em, pb, cst, chunks, sqs, rows, bc_mu, bc_r, TT)
        st_toks = []
        if out_tok is None:
            for c in range(DC):
                sl = slice(c * TT, (c + 1) * TT)
                v("tensor_sub", ya[:, sl], ya[:, sl], bc_mu)
                v("tensor_mul", ya[:, sl], ya[:, sl], bc_r, inc="v")
                em.wait("gpsimd", em.tok("v"))
                em.wait("gpsimd", g("tensor_add", ya[:, sl], ya[:, sl],
                                    xa[:, sl], inc="g"))
                st_toks.append(em.dma("gpsimd",
                                      xout[c * P:(c + 1) * P, t0:t0 + TT],
                                      ya[:, sl]))
        else:
            for c in range(DC):
                sl = slice(c * TT, (c + 1) * TT)
                v("tensor_sub", ya[:, sl], ya[:, sl], bc_mu)
                v("tensor_mul", ya[:, sl], ya[:, sl], bc_r)
                v("tensor_add", ya[:, sl], ya[:, sl], xa[:, sl],
                  inc=("v" if c == DC - 1 else None))
            vfence()
            if em.dbg is not None and it == 0:
                em.wait("gpsimd", em.tok("v"))
                dtk = []
                for c in range(DC):
                    dtk.append(em.dma("gpsimd",
                                      em.dbg["cs"][c * P:(c + 1) * P, 0:TT],
                                      ya[:, c * TT:(c + 1) * TT]))
                em.waits("vector", dtk)
                em.waits("tensor", dtk)
            em.wait("tensor", em.tok("v"))
            for tc in range(NTC):
                for c in range(DC):
                    bankT = pb[4 + (c % 2)]
                    prev = (tc, c - 2) if c >= 2 else (tc - 1, DC - 2)
                    em.wait("tensor", obk_tok.get(prev))
                    t("transpose", bankT[:, 0:P],
                      ya[:, c * TT + tc * P:c * TT + (tc + 1) * P],
                      cst["ident_f32"], inc="t")
                    em.wait("vector", em.tok("t"))
                    v("tensor_copy", obuf[:, tc * 1024 + c * P:
                                          tc * 1024 + (c + 1) * P],
                      bankT[:, 0:P], inc="v")
                    obk_tok[(tc, c)] = em.tok("v")
                em.wait("gpsimd", em.tok("v"))
                if em.dbg is not None and it == 0:
                    em.dma("gpsimd", em.dbg["dt"][tc * P:(tc + 1) * P, 0:512],
                           obuf[:, tc * 1024:tc * 1024 + 512])
                st_toks.append(em.dma("gpsimd",
                                      out_tok[t0 + tc * P:t0 + (tc + 1) * P, :],
                                      obuf[:, tc * 1024:(tc + 1) * 1024]))
            em.waits("vector", st_toks)
    ar.pop()
    return st_toks


def host_inputs_stage4(gg):
    h = {}
    h["gateT"] = np.ascontiguousarray(gg["L1_ffn_gate_W"].T)
    h["wiT"] = np.ascontiguousarray(gg["L1_ffn_wi_W"].T)
    h["woT"] = np.ascontiguousarray(gg["L1_ffn_wo_W"].T)
    h["loraAT"] = np.ascontiguousarray(
        gg["L1_ffn_lora_A"].transpose(2, 0, 1).reshape(1024, 32))
    h["loraBT"] = np.ascontiguousarray(
        gg["L1_ffn_lora_B"].transpose(0, 2, 1).reshape(32, 4096))
    h["gateb_bc"] = np.ascontiguousarray(
        np.tile(gg["L1_ffn_gate_b"][None, :], (P, 1)))
    h["wib_c"] = np.ascontiguousarray(gg["L1_ffn_wi_b"].reshape(32, P).T)
    h["wob_c"] = np.ascontiguousarray(gg["L1_ffn_wo_b"].reshape(DC, P).T)
    h["identb_d"] = np.eye(128, dtype=ml_dtypes.bfloat16)
    return h



def emit_final_transpose(em, ar, pb, cst, xin, out, intoks):
    ar.push()
    buf = ar.f32(1024)
    tp = ar.f32(1024)
    idf = cst["ident_f32"]
    em.waits("sync", intoks)
    sttok = None
    for tc in range(T // P):
        prev = []
        for c in range(DC):
            if sttok is not None and tc > 0:
                pass
            tk = em.dma("sync", buf[:, c * P:(c + 1) * P],
                        xin[c * P:(c + 1) * P, tc * P:(tc + 1) * P])
            em.wait("tensor", tk)
            em.wait("tensor", em.tok("v"))
            em.op("tensor", "transpose", pb[4 + (c % 2)][:, 0:P],
                  buf[:, c * P:(c + 1) * P], idf, inc="t")
            em.wait("vector", em.tok("t"))
            if sttok is not None:
                em.wait("vector", sttok)
            em.op("vector", "tensor_copy", tp[:, c * P:(c + 1) * P],
                  pb[4 + (c % 2)][:, 0:P], inc="v")
        em.wait("gpsimd", em.tok("v"))
        sttok = em.dma("gpsimd", out[tc * P:(tc + 1) * P, :], tp)
        em.wait("sync", em.tok("v"))
    em.wait("gpsimd", sttok)
    ar.pop()



def build_program(stages=(1,), debug=False):
    nc = bass.Bass("TRN2", target_bir_lowering=False, debug=False)
    es = ExitStack()
    xT = nc.dram_tensor("xT", [D, T], F32, kind="ExternalInput").ap()
    out = nc.dram_tensor("out", [T, D], F32, kind="ExternalOutput").ap()
    xd = [nc.dram_tensor(f"xd{i}", [D, T], F32, kind="Internal").ap()
          for i in range(2)]

    W = {}
    def win(name, shape, dt=F32):
        W[name] = nc.dram_tensor(name, list(shape), dt, kind="ExternalInput").ap()

    win("identf", (P, P))
    if 1 in stages:
        win("WxT", (1024, 2048)); win("WdtT", (1024, 1024))
        win("Alog_c", (128, 8)); win("bx_cols_d", (128, 16))
        win("bdt_cols_d", (128, 8))
    if 2 in stages:
        win("w1T", (1024, 4096)); win("w2T", (1024, 4096))
        win("h1T", (4096, 4096)); win("h2T", (4096, 4096))
        win("w3T", (4096, 1024))
        for n in ("w1s_c", "w2s_c", "h1s_c", "h2s_c"):
            win(n, (128, 32))
        win("w3s_c", (128, 8))
    if 3 in stages:
        for n in ("WqT", "WkT", "WvT", "WoT"):
            win(n, (1024, 1024))
        win("betawT", (1024, 16))
        for n in ("rin_c", "rk_c", "rv_c", "bo_c", "lnw_c", "lnb_c"):
            win(n, (128, 8))
        win("betab_c", (16, 1)); win("basebeta_c", (16, 1))
        win("c_esel", (16, 1024)); win("c_epar", (16, 128))
        win("c_d64", (16, 512)); win("c_triu", (128, 128))
    if 3 in stages or 4 in stages:
        win("identb_d", (128, 128), BF16)
    if 4 in stages:
        win("gateT", (1024, 8)); win("wiT", (1024, 4096))
        win("woT", (4096, 1024))
        win("loraAT", (1024, 32)); win("loraBT", (32, 4096))
        win("gateb_bc", (128, 8)); win("wib_c", (128, 32))
        win("wob_c", (128, 8))

    with es:
        ar = Arena(nc, es, 51500)
        pb = [es.enter_context(nc.psum_tensor(f"pb{i}", [P, 512], F32))
              for i in range(7)]
        pbT = es.enter_context(nc.psum_tensor("pbT", [P, 1024], BF16))
        em = EM(nc, es)
        em.dbg = None
        if debug:
            em.dbg = {}
            for nm in ("dt", "b", "tc", "S", "Pf", "cs", "y", "psum0", "rsx",
                       "sbc", "colw", "wq0"):
                em.dbg[nm] = nc.dram_tensor("dbg_" + nm, [D, 512], F32,
                                            kind="ExternalOutput").ap()
            em.dbg["k"] = nc.dram_tensor("dbg_k", [D, 512], BF16,
                                         kind="ExternalOutput").ap()

        cst = {}
        cwt = ar.f32(4)
        orow = ar.f32(128)
        idf = ar.f32(128)
        cst["ones_c"] = cwt[:, 0:1]
        cst["ones_r"] = orow[0:1, 0:128]
        cst["ident_f32"] = idf
        em.op("vector", "memset", cwt[:, 0:1], 1.0)
        em.op("vector", "memset", orow[0:1, 0:128], 1.0, inc="v")
        tki = em.dma("sync", idf, W["identf"])
        em.wait("tensor", tki)
        if "identb_d" in W:
            idb = ar.bf16(128)
            cst["identb"] = idb
            tki = em.dma("sync", idb, W["identb_d"])
            em.wait("tensor", tki)

        if 1 in stages:
            bxc = ar.f32(16)
            bdtc = ar.f32(8)
            t1 = em.dma("sync", bxc, W["bx_cols_d"])
            t2 = em.dma("sync", bdtc, W["bdt_cols_d"])
            em.waits("gpsimd", [t1, t2])
            em.waits("scalar", [t1, t2])
            em.waits("vector", [t1, t2])
            W["bx_cols"] = bxc
            W["bdt_cols"] = bdtc

        cur = xT
        toks = []
        if 2 in stages:
            kind2 = "ExternalOutput" if debug else "Internal"
            ud = nc.dram_tensor("ud", [4096, T], BF16, kind=kind2).ap()
            hmd = nc.dram_tensor("hmd", [4096, T], BF16, kind=kind2).ap()
            hfd = nc.dram_tensor("hfd", [4096, T], BF16, kind=kind2).ap()
        if 1 in stages:
            toks = emit_stage1(em, ar, pb, cst, cur, xd[0], W)
            cur = xd[0]
        if 2 in stages:
            em.waits("sync", toks)
            toks = emit_stage2(em, ar, pb, cst, cur, xd[1], W, ud, hmd, hfd)
            cur = xd[1]
        if 3 in stages:
            em.waits("sync", toks)
            dst = xd[0] if cur is not xd[0] else xd[1]
            toks = emit_stage3(em, ar, pb, pbT, cst, cur, dst, W)
            cur = dst
        if 4 in stages:
            em.waits("sync", toks)
            dst = xd[0] if cur is not xd[0] else xd[1]
            toks = emit_stage4(em, ar, pb, pbT, cst, cur, dst, W, out_tok=out)
            em.waits("gpsimd", toks)
        else:
            emit_final_transpose(em, ar, pb, cst, cur, out, toks)

        with nc.Block() as block:
            em.assemble(block)
    return nc


def host_inputs_stage1(g):
    h = {}
    h["WxT"] = np.ascontiguousarray(g["L0_attn_W_x"].T)
    h["WdtT"] = np.ascontiguousarray(g["L0_attn_W_dt"].T)
    h["Alog_c"] = np.ascontiguousarray(g["L0_attn_A_log"].reshape(DC, P).T)
    h["bx_cols_d"] = np.ascontiguousarray(g["L0_attn_b_x"].reshape(16, P).T)
    h["bdt_cols_d"] = np.ascontiguousarray(g["L0_attn_b_dt"].reshape(DC, P).T)
    return h


_NC_CACHE = {}


def kernel(**inputs):
    if "prog" not in _NC_CACHE:
        _NC_CACHE["prog"] = build_program((1, 2, 3, 4))
    nc = _NC_CACHE["prog"]
    h = {"identf": np.eye(P, dtype=np.float32)}
    h.update(host_inputs_stage1(inputs))
    h.update(host_inputs_stage2(inputs))
    h.update(host_inputs_stage3(inputs))
    h.update(host_inputs_stage4(inputs))
    x = np.asarray(inputs["L0_attn_x"], dtype=np.float32)
    in_maps = []
    for b in range(8):
        m = dict(h)
        m["xT"] = np.ascontiguousarray(x[b].T)
        in_maps.append(m)
    res = run_bass_kernel_spmd(nc, in_maps, core_ids=list(range(8)))
    return np.stack([np.asarray(res.results[b]["out"]) for b in range(8)])
